# version 15
# baseline (speedup 1.0000x reference)
import contextlib
import dataclasses
import numpy as np
import ml_dtypes
import concourse.bass as bass
import concourse.mybir as mybir
from concourse.bass_utils import run_bass_kernel_spmd

F32 = mybir.dt.float32
BF16 = mybir.dt.bfloat16
I32 = mybir.dt.int32
AF = mybir.ActivationFunctionType
ALU = mybir.AluOpType
AXX = mybir.AxisListType.X

S = 8192
DM = 1024
WIN = 2304
NT = 18
NH = 4
OFFS = [0, 1920, 3968, 5888]
R0 = [0, 128, 128, 256]

ENGINES = ("tensor", "scalar", "vector", "gpsimd", "sync")


class Sched:
    def __init__(self):
        self.ops = {e: [] for e in ENGINES}
        self.cnt = {}

    def emit(self, eng, fn):
        self.ops[eng].append(fn)

    def inc(self, sem, k=1):
        self.cnt[sem] = self.cnt.get(sem, 0) + k
        return self.cnt[sem]


def build_program(dump=()):
    nc = bass.Bass("TRN2", target_bir_lowering=False)

    def din(name, shape, dt=F32):
        return nc.dram_tensor(name, shape, dt, kind="ExternalInput").ap()

    x_full = din("x_full", [S, DM])
    x_slice = din("x_slice", [WIN, DM])
    wqT_n = din("wqT_n", [DM, 256]); wqT_r = din("wqT_r", [DM, 256])
    wkT_n = din("wkT_n", [DM, 256]); wkT_r = din("wkT_r", [DM, 256])
    wvT = din("wvT", [DM, 256])
    woT = din("woT", [DM, DM])
    FrT = din("FrT", [DM, DM]); FkT = din("FkT", [DM, DM]); FvT = din("FvT", [DM, DM])
    WbI = din("Wb", [DM, DM])
    mu_r_in = din("mu_r", [128, 8]); mu_k_in = din("mu_k", [128, 8])
    rope_cos = din("rope_cos", [128, S], BF16)
    rope_sgn = din("rope_sgn", [128, S], BF16)
    tri_in = din("tri", [128, 128], BF16)
    bmask_in = din("bmask", [3, 128, 384], BF16)
    ident_in = din("ident", [128, 128], BF16)
    goff = din("goff", [1, 1], I32)

    out = nc.dram_tensor("out", [WIN, DM], F32, kind="ExternalOutput").ap()

    dumps = {}
    def dout(name, shape, dt):
        dumps[name] = nc.dram_tensor("d_" + name, shape, dt, kind="ExternalOutput").ap()
        return dumps[name]

    y1n_d = nc.dram_tensor("y1n_d", [S, DM], BF16).ap()
    yb_d = nc.dram_tensor("yb_d", [WIN, DM], BF16).ap()
    ag_in = nc.dram_tensor("ag_in", [256, S], BF16).ap()
    ag1_out = nc.dram_tensor("ag1_out", [512, S], BF16).ap()
    ag2_out = nc.dram_tensor("ag2_out", [512, S], BF16).ap()

    sch = Sched()
    es = contextlib.ExitStack()
    sems = {}

    def W(eng, s, v):
        if v <= 0:
            return
        sch.emit(eng, lambda E, h=sems[s], v=v: E.wait_ge(h, v))

    def DMA(eng, s, o, i, **kw):
        v = sch.inc(s, 16)
        sch.emit(eng, lambda E, o=o, i=i, h=sems[s], kw=kw: E.dma_start(out=o, in_=i, **kw).then_inc(h, 16))
        return v

    def OP(eng, fn):
        sch.emit(eng, fn)

    def OPI(eng, s, fn):
        v = sch.inc(s, 1)
        sch.emit(eng, lambda E, fn=fn, h=sems[s]: fn(E).then_inc(h, 1))
        return v

    def OPIW(eng, s, fn):
        v = OPI(eng, s, fn)
        W(eng, s, v)
        return v

    with es:
        for sname in ("SQ", "SG", "PE", "AC", "DV", "PL", "CCA", "CCB"):
            sems[sname] = es.enter_context(nc.semaphore(sname))

        def sb(name, shape, dt):
            return es.enter_context(nc.sbuf_tensor(name, shape, dt))

        tri = sb("tri_sb", [128, 128], BF16)
        ident = sb("ident_sb", [128, 128], BF16)
        ones1 = sb("ones1", [1, 128], F32)
        bias3 = sb("bias3", [128, 1], F32)
        eps_t = sb("eps_t", [128, 1], F32)
        mu_r = sb("mu_r_sb", [128, 8], F32); mu_k = sb("mu_k_sb", [128, 8], F32)
        om_r = sb("om_r", [128, 8], F32); om_k = sb("om_k", [128, 8], F32)
        off_sb = sb("off_sb", [1, 1], I32)
        negm3 = sb("negm3", [128, NT], F32)
        rstd3 = sb("rstd3", [128, NT], F32)
        negm_s = sb("negm_s", [128, NT], F32)
        var_s = sb("var_s", [128, NT], F32)
        rstd_s = sb("rstd_s", [128, NT], F32)
        bnst = sb("bnst", [128, 12], F32)
        bnsc = sb("bnsc", [128, 8], F32)
        etri = sb("etri", [128, 512], BF16)

        es_a = contextlib.ExitStack()
        def sba(name, shape, dt):
            return es_a.enter_context(nc.sbuf_tensor(name, shape, dt))
        wq_n = sba("wq_n", [128, 8, 256], BF16)
        wq_r = sba("wq_r", [128, 8, 256], BF16)
        wk_n = sba("wk_n", [128, 8, 256], BF16)
        wk_r = sba("wk_r", [128, 8, 256], BF16)
        wv_s = sba("wv_s", [128, 8, 256], BF16)
        QT = [sba(f"QT{g}", [128, S], BF16) for g in range(2)]
        KT = [sba(f"KT{g}", [128, S], BF16) for g in range(2)]
        Vaug = [sba(f"Vaug{h}", [128, 64 * 65], BF16) for h in range(NH)]

        psA = es.enter_context(nc.psum_tensor("psA", [128, 2048], F32))
        psB = es.enter_context(nc.psum_tensor("psB", [128, 512], F32))
        psD = es.enter_context(nc.psum_tensor("psD", [128, 1024], F32))
        psE = es.enter_context(nc.psum_tensor("psE", [128, 512], BF16))

        blockctx = es.enter_context(nc.Block())

        for dst, src in ((wq_n, wqT_n), (wq_r, wqT_r), (wk_n, wkT_n), (wk_r, wkT_r), (wv_s, wvT)):
            sg_w0 = DMA("gpsimd", "SG", dst[:, :, :], src.rearrange("(c p) d -> p c d", p=128))
        DMA("sync", "SQ", tri[:, :], tri_in)
        DMA("sync", "SQ", ident[:, :], ident_in)
        DMA("sync", "SQ", mu_r[:, :], mu_r_in)
        DMA("sync", "SQ", mu_k[:, :], mu_k_in)
        sq_c0 = DMA("sync", "SQ", off_sb[:, :], goff)

        def _memsets(E):
            E.memset(ones1[:, :], 1.0)
            E.memset(ones1b[:, :], 1.0)
            E.memset(bias3[:, :], -3.0)
            for h in range(NH):
                E.memset(Vaug[h][:, :], 1.0)
            return E.memset(eps_t[:, :], 1e-5)
        OPI("vector", "DV", _memsets)
        W("vector", "SQ", sq_c0)
        OP("vector", lambda E: E.tensor_scalar(om_r[:, :], mu_r[:, :], -1.0, 1.0, ALU.mult, ALU.add))
        dv_c0 = OPI("vector", "DV", lambda E: E.tensor_scalar(om_k[:, :], mu_k[:, :], -1.0, 1.0, ALU.mult, ALU.add))

        def ln_stats(xap, col, negm_t, var_t):
            def s1(E, xap=xap):
                E.bn_stats(bnst[:, 0:6], xap[:, 0:512])
                return E.bn_stats(bnst[:, 6:12], xap[:, 512:1024])
            OPIW("vector", "DV", s1)
            def s2(E, col=col, negm_t=negm_t, var_t=var_t):
                E.reduce_sum(negm_t[:, col:col + 1], bnst[:, 1:12:3], axis=AXX, negate=True)
                E.reduce_sum(var_t[:, col:col + 1], bnst[:, 2:12:3], axis=AXX)
                return E.tensor_mul(bnsc[:, 1:4], bnst[:, 4:12:3], bnst[:, 4:12:3])
            OPIW("vector", "DV", s2)
            def s3(E, col=col, negm_t=negm_t, var_t=var_t):
                E.reduce_sum(bnsc[:, 4:5], bnsc[:, 1:4], axis=AXX)
                E.tensor_mul(bnsc[:, 5:6], bnst[:, 1:2], bnst[:, 1:2])
                E.tensor_mul(bnsc[:, 6:7], negm_t[:, col:col + 1], negm_t[:, col:col + 1])
                return E.tensor_scalar_mul(var_t[:, col:col + 1], var_t[:, col:col + 1], 1.0 / 1024.0)
            OPIW("vector", "DV", s3)
            def s4(E, col=col, negm_t=negm_t):
                E.tensor_add(bnsc[:, 7:8], bnsc[:, 4:5], bnsc[:, 5:6])
                E.tensor_scalar_mul(negm_t[:, col:col + 1], negm_t[:, col:col + 1], 0.25)
                return E.tensor_scalar_mul(bnsc[:, 6:7], bnsc[:, 6:7], 1.0 / 16.0)
            OPIW("vector", "DV", s4)
            def s5(E, col=col, var_t=var_t):
                return E.scalar_tensor_tensor(bnsc[:, 7:8], bnsc[:, 7:8], 0.25,
                                              var_t[:, col:col + 1], ALU.mult, ALU.add)
            OPIW("vector", "DV", s5)
            def s6(E, col=col, var_t=var_t):
                return E.tensor_sub(var_t[:, col:col + 1], bnsc[:, 7:8], bnsc[:, 6:7])
            return OPI("vector", "DV", s6)

        def rstd_batch(var_t, rstd_t, n):
            def _f1(E, var_t=var_t, rstd_t=rstd_t, n=n):
                return E.activation(rstd_t[:, 0:n], var_t[:, 0:n], AF.Ln, bias=eps_t[:, :], scale=1.0)
            OPIW("scalar", "AC", _f1)
            def _f2(E, rstd_t=rstd_t, n=n):
                return E.activation(rstd_t[:, 0:n], rstd_t[:, 0:n], AF.Exp, bias=0.0, scale=-0.5)
            return OPI("scalar", "AC", _f2)

        rs_sb = [sba(f"rs{i}", [1, 512], F32) for i in range(4)]
        rsb_sb = [sba(f"rsb{i}", [1, 512], BF16) for i in range(4)]
        ones1b = sba("ones1b", [1, 64], BF16)
        ou_sb = [sba(f"ou{i}", [64, 512], F32) for i in range(4)]
        ostg = [sba(f"ostg{i}", [64, 512], BF16) for i in range(4)]
        Eb_s = [sba(f"Ebuf{i}", [128, 1024], BF16) for i in range(2)]

        pv_hist = [0, 0]
        exp_hist = [0, 0]
        msk_dv = [0, 0]
        psd_free = [dv_c0, 0]
        norm_dv = [dv_c0] * 4
        stg_dma = [0] * 4
        stg_last = [0, 0]
        kv_pl = 0
        kv_ac = 0
        deferred = []

        def emit_attn_batch(qbs):
            W("tensor", "PL", kv_pl)
            W("tensor", "DV", kv_ac)
            for g2 in range(2):
                for qb in qbs:
                    qs = 512 * qb
                    nkb = 4 * (qb + 1)
                    pend = None

                    def emit_pv(args, last, qb=qb):
                        (kb, reg, diag, col0, first, eb) = args
                        W("tensor", "AC", exp_hist[reg])
                        if diag:
                            W("tensor", "DV", msk_dv[reg])
                        if first:
                            W("tensor", "DV", psd_free[0])
                        def pvmm(E, g2=g2, kb=kb, diag=diag, col0=col0, eb=eb,
                                 first=first, last=last):
                            r = None
                            for hh in range(2):
                                hcol = 512 * hh
                                V = Vaug[2 * g2 + hh]
                                if diag:
                                    r = E.matmul(psD[0:65, hcol + col0:hcol + col0 + 128],
                                                 V[:, 65 * kb:65 * kb + 65],
                                                 etri[:, 256 * (kb % 2) + 128 * hh:256 * (kb % 2) + 128 * hh + 128],
                                                 start=first, stop=last)
                                    if col0 + 128 < 512:
                                        r = E.matmul(psD[0:65, hcol + col0 + 128:hcol + 512],
                                                     V[:, 65 * kb:65 * kb + 65],
                                                     eb[:, hcol + col0 + 128:hcol + 512],
                                                     start=first, stop=False)
                                else:
                                    r = E.matmul(psD[0:65, hcol:hcol + 512],
                                                 V[:, 65 * kb:65 * kb + 65],
                                                 eb[:, hcol:hcol + 512],
                                                 start=first, stop=False)
                            return r
                        v = OPI("tensor", "PE", pvmm)
                        pv_hist[reg] = v
                        return v

                    for kb in range(nkb):
                        reg = kb % 2
                        r0c = 1024 * reg
                        diag = kb >= 4 * qb
                        col0 = 128 * (kb - 4 * qb) if diag else 0
                        W("tensor", "AC", exp_hist[reg])
                        def smm(E, g2=g2, kb=kb, qs=qs, col0=col0, r0c=r0c):
                            E.matmul(psA[:, r0c + col0:r0c + 512],
                                     KT[g2][0:64, 128 * kb:128 * kb + 128],
                                     QT[g2][0:64, qs + col0:qs + 512],
                                     start=True, stop=True)
                            return E.matmul(psA[:, r0c + 512 + col0:r0c + 1024],
                                            KT[g2][64:128, 128 * kb:128 * kb + 128],
                                            QT[g2][64:128, qs + col0:qs + 512],
                                            start=True, stop=True)
                        pe_s = OPI("tensor", "PE", smm)
                        if kb == 1 and deferred:
                            for fn in deferred:
                                fn()
                            deferred.clear()
                        W("scalar", "PE", pe_s)
                        W("scalar", "PE", pv_hist[reg])
                        eb = Eb_s[reg]
                        ac_e = OPI("scalar", "AC", lambda E, eb=eb, r0c=r0c: E.activation(
                            eb[:, 0:1024], psA[:, r0c:r0c + 1024], AF.Exp,
                            bias=bias3[:, :], scale=0.125))
                        exp_hist[reg] = ac_e
                        if diag:
                            W("vector", "AC", ac_e)
                            W("vector", "PE", pv_hist[reg])
                            def msk(E, eb=eb, col0=col0, kb=kb):
                                E.tensor_mul(etri[:, 256 * (kb % 2):256 * (kb % 2) + 128],
                                             eb[:, col0:col0 + 128], tri[:, :])
                                return E.tensor_mul(etri[:, 256 * (kb % 2) + 128:256 * (kb % 2) + 256],
                                                    eb[:, 512 + col0:512 + col0 + 128], tri[:, :])
                            msk_dv[reg] = OPI("vector", "DV", msk)
                        if pend is not None:
                            emit_pv(pend, last=False)
                        pend = (kb, reg, diag, col0, kb == 0, eb)
                    pv_last = emit_pv(pend, last=True)

                    dv_rs = [0, 0]
                    ac_os = [0, 0]
                    for hh in range(2):
                        slot = 2 * (qb % 2) + hh
                        hcol = 512 * hh
                        W("vector", "PE", pv_last)
                        W("vector", "DV", norm_dv[slot])
                        OPIW("vector", "DV", lambda E, slot=slot, hcol=hcol: E.tensor_copy(
                            rs_sb[slot][:, :], psD[64:65, hcol:hcol + 512]))
                        OPIW("vector", "DV", lambda E, slot=slot: E.reciprocal(
                            rs_sb[slot][:, :], rs_sb[slot][:, :]))
                        dv_rs[hh] = OPI("vector", "DV", lambda E, slot=slot: E.tensor_copy(
                            rsb_sb[slot][:, :], rs_sb[slot][:, :]))
                        ac_os[hh] = OPI("vector", "DV", lambda E, slot=slot, hcol=hcol: E.tensor_copy(
                            ou_sb[slot][:, :], psD[0:64, hcol:hcol + 512]))
                    psd_free[0] = sch.cnt["DV"]
                    psd_free[1] = 0

                    def mk_tail(qb=qb, g2=g2, qs=qs, dv_rs=tuple(dv_rs), ac_os=tuple(ac_os)):
                        def tailf():
                            for hh in range(2):
                                slot = 2 * (qb % 2) + hh
                                W("tensor", "DV", dv_rs[hh])
                                W("tensor", "DV", norm_dv[slot])
                                pe_bc = OPI("tensor", "PE", lambda E, slot=slot, hh=hh: E.matmul(
                                    psB[64 * hh:64 * hh + 64, :], ones1b[:, :],
                                    rsb_sb[slot][:, :], start=True, stop=True))
                                W("vector", "PE", pe_bc)
                                W("vector", "DV", ac_os[hh])
                                W("vector", "SQ", stg_dma[slot])
                                def norm(E, hh=hh, slot=slot):
                                    return E.tensor_mul(ostg[slot][:, :],
                                                        ou_sb[slot][:, :], psB[64 * hh:64 * hh + 64, :])
                                norm_dv[slot] = OPI("vector", "DV", norm)
                                W("sync", "DV", norm_dv[slot])
                                stg_dma[slot] = DMA("sync", "SQ",
                                                    ag_in[128 * g2 + 64 * hh:128 * g2 + 64 * hh + 64, qs:qs + 512],
                                                    ostg[slot][:, :])
                                stg_last[g2] = stg_dma[slot]
                        return tailf
                    deferred.append(mk_tail())

        with contextlib.ExitStack() as es12:
            xch = [es12.enter_context(nc.sbuf_tensor(f"xch{i}", [128, 1024], F32)) for i in range(2)]
            xap = [es12.enter_context(nc.sbuf_tensor(f"xap{i}", [128, 1024], F32)) for i in range(2)]
            ych = [es12.enter_context(nc.sbuf_tensor(f"ych{i}", [128, 1024], BF16)) for i in range(2)]
            y1nT = [es12.enter_context(nc.sbuf_tensor(f"y1nT{c}", [128, 1024], BF16)) for c in range(8)]
            cos_st = es12.enter_context(nc.sbuf_tensor("cos_st", [128, 1024], BF16))
            sgn_st = es12.enter_context(nc.sbuf_tensor("sgn_st", [128, 1024], BF16))
            rsc = [[es12.enter_context(nc.sbuf_tensor(f"rsc{p}_{i}", [128, 512], BF16))
                    for i in range(4)] for p in range(2)]
            rtmp1 = es12.enter_context(nc.sbuf_tensor("rtmp1", [128, 512], F32))
            rtmp2 = es12.enter_context(nc.sbuf_tensor("rtmp2", [128, 512], F32))
            bnq = es12.enter_context(nc.sbuf_tensor("bnq", [128, 16, 12], F32))
            msq = es12.enter_context(nc.sbuf_tensor("msq", [128, 16, 4], F32))
            s2q = es12.enter_context(nc.sbuf_tensor("s2q", [128, 16], F32))
            w1q = es12.enter_context(nc.sbuf_tensor("w1q", [128, 16], F32))
            uq = es12.enter_context(nc.sbuf_tensor("uq", [128, 16], F32))

            xch_free = [0, 0]
            xap_free = [0, 0]
            ych_free = [0, 0]
            last_y1w = 0
            rope_done = 0
            evac_hist = [0, 0]
            rope_pl = [0, 0]
            v_evac_h = [0, 0]
            pe_qkv_done = 0

            def emit_ln_quarter(qs_):
                nonlocal last_y1w
                st_dv = 0
                for ct in range(16):
                    gc = 16 * qs_ + ct
                    bi = gc % 2
                    W("sync", "DV", xch_free[bi])
                    xv = DMA("sync", "SQ", xch[bi][:, :], x_full[128 * gc:128 * (gc + 1), :])
                    W("vector", "SQ", xv)
                    def st(E, bi=bi, ct=ct):
                        E.bn_stats(bnq[:, ct, 0:6], xch[bi][:, 0:512])
                        return E.bn_stats(bnq[:, ct, 6:12], xch[bi][:, 512:1024])
                    st_dv = OPI("vector", "DV", st)
                    xch_free[bi] = st_dv
                W("vector", "DV", st_dv)
                W("vector", "PL", max(xap_free))
                def cb1(E):
                    for t in range(16):
                        E.reduce_sum(negm_s[:, t:t + 1], bnq[:, t, 1:12:3], axis=AXX, negate=True)
                        E.reduce_sum(var_s[:, t:t + 1], bnq[:, t, 2:12:3], axis=AXX)
                        r = E.tensor_mul(msq[:, t, :], bnq[:, t, 1:12:3], bnq[:, t, 1:12:3])
                    return r
                OPIW("vector", "DV", cb1)
                def cb2(E):
                    for t in range(16):
                        r = E.reduce_sum(s2q[:, t:t + 1], msq[:, t, :], axis=AXX)
                    E.tensor_mul(uq[:, :], negm_s[:, 0:16], negm_s[:, 0:16])
                    return E.tensor_scalar_mul(w1q[:, :], var_s[:, 0:16], 1.0 / 1024.0)
                OPIW("vector", "DV", cb2)
                def cb3(E):
                    E.tensor_scalar_mul(negm_s[:, 0:16], negm_s[:, 0:16], 0.25)
                    return E.scalar_tensor_tensor(s2q[:, :], s2q[:, :], 0.25, w1q[:, :], ALU.mult, ALU.add)
                OPIW("vector", "DV", cb3)
                def cb4(E):
                    return E.scalar_tensor_tensor(var_s[:, 0:16], uq[:, :], -1.0 / 16.0,
                                                  s2q[:, :], ALU.mult, ALU.add)
                cb_dv = OPI("vector", "DV", cb4)
                W("scalar", "DV", cb_dv)
                ac_r = rstd_batch(var_s, rstd_s, 16)
                W("gpsimd", "DV", cb_dv)
                W("gpsimd", "AC", ac_r)
                for ct in range(16):
                    gc = 16 * qs_ + ct
                    bi = gc % 2
                    W("sync", "PL", xap_free[bi])
                    xv = DMA("sync", "SQ", xap[bi][:, :], x_full[128 * gc:128 * (gc + 1), :])
                    W("gpsimd", "SQ", xv)
                    W("gpsimd", "SQ", ych_free[bi])
                    pl_ap = OPI("gpsimd", "PL", lambda E, bi=bi, ct=ct: E.tensor_scalar(
                        ych[bi][:, :], xap[bi][:, :],
                        negm_s[:, ct:ct + 1], rstd_s[:, ct:ct + 1], ALU.add, ALU.mult))
                    xap_free[bi] = pl_ap
                    W("sync", "PL", pl_ap)
                    yv = DMA("sync", "SQ", y1n_d[128 * gc:128 * (gc + 1), :], ych[bi][:, :])
                    ych_free[bi] = yv
                    last_y1w = yv

            emit_ln_quarter(0)
            for qt in range(4):
                for half in range(2):
                    eh = 2 * qt + half
                    W("sync", "SQ", last_y1w)
                    W("sync", "PE", pe_qkv_done)
                    for cc in range(8):
                        tv = DMA("sync", "SQ", y1nT[cc][:, :],
                                 y1n_d[1024 * eh:1024 * (eh + 1), 128 * cc:128 * (cc + 1)], transpose=True)
                    W("sync", "PL", rope_done)
                    DMA("sync", "SQ", cos_st[:, :], rope_cos[:, 1024 * eh:1024 * (eh + 1)])
                    tab_done = DMA("sync", "SQ", sgn_st[:, :], rope_sgn[:, 1024 * eh:1024 * (eh + 1)])

                    W("tensor", "SQ", tab_done)
                    if qt == 0 and half == 0:
                        W("tensor", "SG", sg_w0)
                    W("tensor", "AC", max(exp_hist))
                    for g2 in range(2):
                        for nb in range(2):
                            par = nb % 2
                            W("tensor", "DV", evac_hist[0])
                            def qmm(E, g2=g2, nb=nb):
                                for sl, w in ((0, wq_n), (1, wq_r)):
                                    for cc in range(8):
                                        r = E.matmul(psA[:, 512 * sl:512 * (sl + 1)],
                                                     w[:, cc, 128 * g2:128 * g2 + 128],
                                                     y1nT[cc][:, 512 * nb:512 * nb + 512],
                                                     start=(cc == 0), stop=(cc == 7))
                                return r
                            pe_q = OPI("tensor", "PE", qmm)
                            W("tensor", "DV", evac_hist[1])
                            def kmm(E, g2=g2, nb=nb):
                                for sl, w in ((2, wk_n), (3, wk_r)):
                                    for cc in range(8):
                                        r = E.matmul(psA[:, 512 * sl:512 * (sl + 1)],
                                                     w[:, cc, 128 * g2:128 * g2 + 128],
                                                     y1nT[cc][:, 512 * nb:512 * nb + 512],
                                                     start=(cc == 0), stop=(cc == 7))
                                return r
                            pe_k = OPI("tensor", "PE", kmm)
                            W("vector", "PE", pe_q)
                            W("vector", "PL", rope_pl[par])
                            def evacq(E, par=par):
                                E.tensor_copy(rsc[par][0][:, :], psA[:, 0:512])
                                return E.tensor_copy(rsc[par][1][:, :], psA[:, 512:1024])
                            evac_hist[0] = OPI("vector", "DV", evacq)
                            W("vector", "PE", pe_k)
                            def evack(E, par=par):
                                E.tensor_copy(rsc[par][2][:, :], psA[:, 1024:1536])
                                return E.tensor_copy(rsc[par][3][:, :], psA[:, 1536:2048])
                            ac_ev = OPI("vector", "DV", evack)
                            evac_hist[1] = ac_ev
                            col = 1024 * eh + 512 * nb
                            W("gpsimd", "DV", ac_ev)
                            def ropem(E, par=par, nb=nb):
                                E.tensor_mul(rtmp1[:, :], rsc[par][0][:, :], cos_st[:, 512 * nb:512 * nb + 512])
                                return E.tensor_mul(rtmp2[:, :], rsc[par][1][:, :], sgn_st[:, 512 * nb:512 * nb + 512])
                            v = OPI("gpsimd", "PL", ropem)
                            W("gpsimd", "PL", v)
                            OPI("gpsimd", "PL", lambda E, g2=g2, col=col: E.tensor_add(
                                QT[g2][:, col:col + 512], rtmp1[:, :], rtmp2[:, :]))
                            def ropem2(E, par=par, nb=nb):
                                E.tensor_mul(rtmp1[:, :], rsc[par][2][:, :], cos_st[:, 512 * nb:512 * nb + 512])
                                return E.tensor_mul(rtmp2[:, :], rsc[par][3][:, :], sgn_st[:, 512 * nb:512 * nb + 512])
                            v = OPI("gpsimd", "PL", ropem2)
                            W("gpsimd", "PL", v)
                            rope_pl[par] = OPI("gpsimd", "PL", lambda E, g2=g2, col=col: E.tensor_add(
                                KT[g2][:, col:col + 512], rtmp1[:, :], rtmp2[:, :]))
                    rope_done = max(rope_pl)

                    W("tensor", "AC", max(exp_hist))
                    for nb in range(2):
                        for tt in range(4):
                            kb = 8 * eh + 4 * nb + tt
                            vslc = psA[:, 0:256] if kb % 2 == 0 else psA[:, 512:768]
                            W("tensor", "DV", v_evac_h[kb % 2])
                            def vmm(E, nb=nb, tt=tt, vslc=vslc):
                                for cc in range(8):
                                    r = E.matmul(vslc,
                                                 y1nT[cc][:, 512 * nb + 128 * tt:512 * nb + 128 * tt + 128],
                                                 wv_s[:, cc, :], start=(cc == 0), stop=(cc == 7))
                                return r
                            pe_v = OPI("tensor", "PE", vmm)
                            W("vector", "PE", pe_v)
                            def vev(E, kb=kb, vslc=vslc):
                                for h in range(NH):
                                    r = E.tensor_copy(Vaug[h][:, 65 * kb:65 * kb + 64],
                                               vslc[:, 64 * h:64 * h + 64])
                                return r
                            v_evac_h[kb % 2] = OPI("vector", "DV", vev)
                    pe_qkv_done = sch.cnt["PE"]
                    if half == 0 and qt < 3:
                        emit_ln_quarter(qt + 1)
                    kv_pl = rope_done
                    kv_ac = max(v_evac_h)
                    emit_attn_batch(range(2 * eh, 2 * eh + 2))

            for fn in deferred:
                fn()
            deferred.clear()

            if "qkv" in dump:
                dqt = dout("QT", [2, 128, S], BF16)
                dkt = dout("KT", [2, 128, S], BF16)
                dva = dout("Vaug", [NH, 128, 64 * 65], BF16)
                W("sync", "PL", rope_done)
                W("sync", "AC", max(v_evac_h))
                for g in range(2):
                    DMA("sync", "SQ", dqt[g], QT[g][:, :])
                    DMA("sync", "SQ", dkt[g], KT[g][:, :])
                for h in range(NH):
                    DMA("sync", "SQ", dva[h], Vaug[h][:, :])
            if "y1" in dump:
                dy1 = dout("y1n", [S, DM], BF16)
                W("sync", "SQ", last_y1w)
                DMA("sync", "SQ", dy1, y1n_d)

        W("gpsimd", "SQ", stg_last[0])
        OPI("gpsimd", "CCA", lambda E: E.collective_compute(
            "AllGather", ALU.bypass,
            replica_groups=[[0, 1, 2, 3], [4, 5, 6, 7]],
            ins=[ag_in[0:128, :].opt()], outs=[ag1_out.opt()]))
        W("gpsimd", "SQ", stg_last[1])
        OPI("gpsimd", "CCB", lambda E: E.collective_compute(
            "AllGather", ALU.bypass,
            replica_groups=[[0, 1, 2, 3], [4, 5, 6, 7]],
            ins=[ag_in[128:256, :].opt()], outs=[ag2_out.opt()]))
        agv = max(stg_dma)

        if "attn" in dump:
            dat = dout("ag_in", [256, S], BF16)
            W("sync", "SQ", agv)
            DMA("sync", "SQ", dat, ag_in)

        es_a.close()
        stream = sb("stream", [128, NT * 1024], F32)
        es5 = contextlib.ExitStack()
        G_sb = [es5.enter_context(nc.sbuf_tensor(f"G{cc}", [128, WIN], BF16)) for cc in range(8)]
        wo_sb = es5.enter_context(nc.sbuf_tensor("wo_sb", [128, 8, 1024], BF16))
        xsl = [es5.enter_context(nc.sbuf_tensor(f"xsl{i}", [128, 1024], F32)) for i in range(2)]

        sg_wo = DMA("gpsimd", "SG", wo_sb[:, :, :], woT.rearrange("(c p) d -> p c d", p=128))

        W("sync", "SQ", sq_c0)
        W("sync", "CCA", 1)
        vG = sch.inc("SQ", 8 * 16)
        def gload_even(E):
            t0 = E.value_load(off_sb[0:1, 0:1])
            for cc in range(0, 8, 2):
                base = ag1_out[128 * (cc // 2):128 * (cc // 2) + 128, 0:WIN]
                dyn = dataclasses.replace(base, offset=base.offset + t0)
                r = E.dma_start(out=G_sb[cc][:, :], in_=dyn).then_inc(sems["SQ"], 16)
            return r
        sch.emit("sync", gload_even)
        W("sync", "CCB", 1)
        def gload_odd(E):
            t0 = E.value_load(off_sb[0:1, 0:1])
            for cc in range(1, 8, 2):
                base = ag2_out[128 * (cc // 2):128 * (cc // 2) + 128, 0:WIN]
                dyn = dataclasses.replace(base, offset=base.offset + t0)
                r = E.dma_start(out=G_sb[cc][:, :], in_=dyn).then_inc(sems["SQ"], 16)
            return r
        sch.emit("sync", gload_odd)

        x1_dv = [dv_c0, dv_c0]
        x1_dv_t = [0] * NT
        for t in range(NT):
            bi = t % 2
            W("sync", "DV", x1_dv[bi])
            xv = DMA("sync", "SQ", xsl[bi][:, :], x_slice[128 * t:128 * (t + 1), :])
            if t == 0:
                W("tensor", "SQ", vG)
                W("tensor", "SG", sg_wo)
            W("tensor", "DV", x1_dv[(t + 1) % 2])
            def womm(E, t=t):
                for cc in range(8):
                    for k in range(2):
                        r = E.matmul(psD[:, 512 * k:512 * (k + 1)],
                                     G_sb[cc][:, 128 * t:128 * t + 128],
                                     wo_sb[:, cc, 512 * k:512 * (k + 1)],
                                     start=(cc == 0), stop=(cc == 7))
                return r
            pe_p5 = OPI("tensor", "PE", womm)
            W("vector", "PE", pe_p5)
            W("vector", "SQ", xv)
            def x1add(E, t=t, bi=bi):
                E.tensor_add(stream[:, 1024 * t:1024 * t + 512], psD[:, 0:512], xsl[bi][:, 0:512])
                return E.tensor_add(stream[:, 1024 * t + 512:1024 * (t + 1)],
                                    psD[:, 512:1024], xsl[bi][:, 512:1024])
            x1_dv[bi] = OPI("vector", "DV", x1add)
            x1_dv_t[t] = x1_dv[bi]

        def dump_stream(name, ready_sem, ready_val):
            dx = dout(name, [WIN, DM], F32)
            W("sync", ready_sem, ready_val)
            vv = DMA("sync", "SQ", dx.rearrange("(n p) c -> p n c", p=128),
                     stream[:, :].rearrange("p (n c) -> p n c", c=1024))
            W("vector", "SQ", vv)
            W("gpsimd", "SQ", vv)

        if "x1" in dump:
            dump_stream("x1", "DV", max(x1_dv))

        _ffn_id = [0]
        def ffn_phase(ready_sem, ready_vals, esf):
            _ffn_id[0] += 1
            _sfx = f"_f{_ffn_id[0]}"
            def sbf(name, shape, dt):
                return esf.enter_context(nc.sbuf_tensor(name + _sfx, shape, dt))
            ynT = sbf("ynT", [128, 8 * 1280], BF16)
            xmix = sbf("xmix", [128, 8 * 1152], BF16)
            kaT = sbf("kaT", [128, 8 * 1152], BF16)
            krl = sbf("krl", [128, 512], BF16)
            ytile = [sbf(f"ytile{i}", [128, 1024], BF16) for i in range(2)]
            rsig = sbf("rsig", [128, 1024], F32)
            mix_sc = sbf("mix_sc", [128, 1152], F32)
            tv_sb = sbf("tv_sb", [128, 1024], F32)
            Fr_sb = sbf("Fr_sb", [128, 8, 1024], BF16)
            Fk_sb = sbf("Fk_sb", [128, 8, 1024], BF16)
            Fv_sb = sbf("Fv_sb", [128, 8, 1024], BF16)

            W("gpsimd", ready_sem, max(ready_vals) if isinstance(ready_vals, (list, tuple)) else ready_vals)
            DMA("gpsimd", "SG", Fk_sb[:, :, :], FkT.rearrange("(c p) d -> p c d", p=128))
            sg_fkr = DMA("gpsimd", "SG", Fr_sb[:, :, :], FrT.rearrange("(c p) d -> p c d", p=128))
            sg_fv = DMA("gpsimd", "SG", Fv_sb[:, :, :], FvT.rearrange("(c p) d -> p c d", p=128))
            rv = ready_vals if isinstance(ready_vals, (list, tuple)) else [ready_vals] * NT
            for t in range(NT):
                W("vector", ready_sem, rv[t])
                dv_st = ln_stats(stream[:, 1024 * t:1024 * (t + 1)], t, negm_s, var_s)
            W("scalar", "DV", dv_st)
            ac_r = rstd_batch(var_s, rstd_s, NT)
            W("vector", "AC", ac_r)
            yt_free = [0, 0]
            ywv_t = [0] * NT
            for t in range(NT):
                bi = t % 2
                W("vector", "SQ", yt_free[bi])
                dv_a = OPI("vector", "DV", lambda E, t=t, bi=bi: E.tensor_scalar(
                    ytile[bi][:, :], stream[:, 1024 * t:1024 * (t + 1)],
                    negm_s[:, t:t + 1], rstd_s[:, t:t + 1], ALU.add, ALU.mult))
                W("sync", "DV", dv_a)
                ywv_t[t] = DMA("sync", "SQ", yb_d[128 * t:128 * (t + 1), :], ytile[bi][:, :])
                yt_free[bi] = ywv_t[t]

            last_pl = 0
            pl_t = [0] * NT
            pe_sp = 0
            ac_sig_prev = 0
            for sp in range(2):
                lo = 0 if sp == 0 else 1024
                span = 1152 if sp == 0 else 1280
                shoff = 0 if sp == 0 else 128
                W("sync", "SQ", ywv_t[8 if sp == 0 else 17])
                W("sync", "DV", sch.cnt["DV"])
                for cc in range(8):
                    tv_ = DMA("sync", "SQ", ynT[:, 1280 * cc:1280 * cc + span],
                              yb_d[lo:lo + span, 128 * cc:128 * (cc + 1)], transpose=True)
                W("vector", "SQ", tv_)
                W("vector", "PE", pe_sp)

                def mix_one(mu, om, shoff=shoff, sp=sp):
                    r = 0
                    for cc in range(8):
                        b = 1280 * cc
                        o = 1152 * cc
                        if sp == 0:
                            def _a(E, b=b, o=o, mu=mu, om=om):
                                E.tensor_scalar(xmix[:, o:o + 1], ynT[:, b:b + 1],
                                                mu[:, cc if False else 0:1], None, ALU.mult)
                                return E.tensor_scalar(mix_sc[:, 0:1151], ynT[:, b:b + 1151],
                                                       om[:, 0:1], None, ALU.mult)
                            def _a(E, b=b, o=o, mu=mu, om=om, cc=cc):
                                E.tensor_scalar(xmix[:, o:o + 1], ynT[:, b:b + 1],
                                                mu[:, cc:cc + 1], None, ALU.mult)
                                return E.tensor_scalar(mix_sc[:, 0:1151], ynT[:, b:b + 1151],
                                                       om[:, cc:cc + 1], None, ALU.mult)
                            OPIW("vector", "DV", _a)
                            def _b(E, b=b, o=o, mu=mu, cc=cc):
                                return E.scalar_tensor_tensor(xmix[:, o + 1:o + 1152],
                                                              ynT[:, b + 1:b + 1152],
                                                              mu[:, cc:cc + 1], mix_sc[:, 0:1151],
                                                              ALU.mult, ALU.add)
                            r = OPI("vector", "DV", _b)
                        else:
                            def _a(E, b=b, mu=mu, om=om, cc=cc, shoff=shoff):
                                return E.tensor_scalar(mix_sc[:, 0:1152],
                                                       ynT[:, b + shoff - 1:b + shoff - 1 + 1152],
                                                       om[:, cc:cc + 1], None, ALU.mult)
                            OPIW("vector", "DV", _a)
                            def _b(E, b=b, o=o, mu=mu, cc=cc, shoff=shoff):
                                return E.scalar_tensor_tensor(xmix[:, o:o + 1152],
                                                              ynT[:, b + shoff:b + shoff + 1152],
                                                              mu[:, cc:cc + 1], mix_sc[:, 0:1152],
                                                              ALU.mult, ALU.add)
                            r = OPI("vector", "DV", _b)
                    return r

                dv_mx = mix_one(mu_k, om_k)
                W("tensor", "DV", dv_mx)
                W("tensor", "AC", ac_sig_prev)
                if sp == 0:
                    W("tensor", "SG", sg_fkr)
                ac_hist = [0, 0]
                dv_k = 0
                for dd in range(8):
                    for ncb in range(3):
                        n0 = 512 * ncb
                        nw = min(512, 1152 - n0)
                        slot = (dd * 3 + ncb) % 2
                        pslc = psB[:, 0:nw] if slot == 0 else psA[:, 0:nw]
                        W("tensor", "AC", ac_hist[slot])
                        def kmm(E, dd=dd, n0=n0, nw=nw, pslc=pslc):
                            for cc in range(8):
                                r = E.matmul(pslc, Fk_sb[:, cc, 128 * dd:128 * dd + 128],
                                             xmix[:, 1152 * cc + n0:1152 * cc + n0 + nw],
                                             start=(cc == 0), stop=(cc == 7))
                            return r
                        pe_k = OPI("tensor", "PE", kmm)
                        W("scalar", "PE", pe_k)
                        W("scalar", "DV", dv_k)
                        ac_k = OPI("scalar", "AC", lambda E, pslc=pslc, nw=nw: E.activation(
                            krl[:, 0:nw], pslc, AF.Relu, bias=0.0, scale=1.0))
                        ac_hist[slot] = ac_k
                        W("vector", "AC", ac_k)
                        dv_k = OPI("vector", "DV", lambda E, dd=dd, n0=n0, nw=nw: E.tensor_mul(
                            kaT[:, 1152 * dd + n0:1152 * dd + n0 + nw],
                            krl[:, 0:nw], krl[:, 0:nw]))
                pe_kdone = sch.cnt["PE"]
                W("vector", "PE", pe_kdone)
                dv_mr = mix_one(mu_r, om_r)
                W("tensor", "DV", dv_k)
                W("tensor", "DV", dv_mr)
                if sp == 0:
                    W("tensor", ready_sem, max(rv))
                    W("tensor", "SG", sg_fv)
                ac_sig = 0
                ac_sig_h = [0, 0]
                dv_rv = 0
                for tl in range(9):
                    t = 9 * sp + tl
                    rp = psD[:, 0:1024] if t % 2 == 0 else psA[:, 0:1024]
                    W("tensor", "AC", ac_sig_h[t % 2])
                    def rmm(E, tl=tl, rp=rp):
                        for cc in range(8):
                            for k in range(2):
                                r = E.matmul(rp[:, 512 * k:512 * (k + 1)],
                                             xmix[:, 1152 * cc + 128 * tl:1152 * cc + 128 * tl + 128],
                                             Fr_sb[:, cc, 512 * k:512 * (k + 1)],
                                             start=(cc == 0), stop=(cc == 7))
                        return r
                    pe_r = OPI("tensor", "PE", rmm)
                    W("scalar", "PE", pe_r)
                    W("scalar", "DV", dv_rv)
                    def sig(E, rp=rp):
                        E.activation(rsig[:, 0:512], rp[:, 0:512], AF.Sigmoid, bias=0.0, scale=1.0)
                        return E.activation(rsig[:, 512:1024], rp[:, 512:1024], AF.Sigmoid, bias=0.0, scale=1.0)
                    ac_sig = OPI("scalar", "AC", sig)
                    ac_sig_h[t % 2] = ac_sig
                    ac_sig_prev = ac_sig
                    W("tensor", "DV", dv_rv)
                    def vmm2(E, tl=tl):
                        for cc in range(8):
                            for k in range(2):
                                r = E.matmul(psA[:, 1024 + 512 * k:1024 + 512 * (k + 1)],
                                             kaT[:, 1152 * cc + 128 * tl:1152 * cc + 128 * tl + 128],
                                             Fv_sb[:, cc, 512 * k:512 * (k + 1)],
                                             start=(cc == 0), stop=(cc == 7))
                        return r
                    pe_v = OPI("tensor", "PE", vmm2)
                    W("vector", "PE", pe_v)
                    W("vector", "AC", ac_sig)
                    W("vector", "PL", last_pl)
                    def rv(E):
                        E.tensor_mul(tv_sb[:, 0:512], psA[:, 1024:1536], rsig[:, 0:512])
                        return E.tensor_mul(tv_sb[:, 512:1024], psA[:, 1536:2048], rsig[:, 512:1024])
                    dv_rv = OPI("vector", "DV", rv)
                    W("gpsimd", "DV", dv_rv)
                    W("gpsimd", "PL", last_pl)
                    last_pl = OPI("gpsimd", "PL", lambda E, t=t: E.tensor_add(
                        stream[:, 1024 * t:1024 * (t + 1)],
                        stream[:, 1024 * t:1024 * (t + 1)], tv_sb[:, :]))
                    pl_t[t] = last_pl
                pe_sp = sch.cnt["PE"]
            return pl_t

        es5.close()
        with contextlib.ExitStack() as es6:
            pl_x2_t = ffn_phase("DV", x1_dv_t, es6)
            pl_x2 = max(pl_x2_t)
        if "x2" in dump:
            dump_stream("x2", "PL", pl_x2)

        es7 = contextlib.ExitStack()
        def sb7(name, shape, dt):
            return es7.enter_context(nc.sbuf_tensor(name, shape, dt))
        y3n = sb7("y3n", [128, NT * 1024], BF16)
        y3nT = sb7("y3nT", [128, 8 * 1408], BF16)
        AT_sb = sb7("AT_sb", [128, 8 * 1408], BF16)
        bmask = sb7("bmask_sb", [128, 3, 384], BF16)
        sc_sb = sb7("sc_sb", [128, 384], F32)
        negb = sb7("negb", [128, 1], F32)
        rcpb = sb7("rcpb", [128, 1], F32)
        EbT = sb7("EbT", [128, 384], BF16)
        ra_sb = sb7("ra_sb", [128, 1024], F32)
        y3f = sb7("y3f", [128, 1024], F32)
        Wb_sb = sb7("Wb_sb", [128, 8, 1024], BF16)

        sg_wb = DMA("gpsimd", "SG", Wb_sb[:, :, :], WbI.rearrange("(c p) d -> p c d", p=128))
        W("sync", "PL", pl_x2)
        bmv = DMA("sync", "SQ", bmask[:, :, :], bmask_in.transpose([1, 0, 2]))

        for t in range(NT):
            W("vector", "PL", pl_x2_t[t])
            dv_st3 = ln_stats(stream[:, 1024 * t:1024 * (t + 1)], t, negm3, var_s)
        W("scalar", "DV", dv_st3)
        ac_r3 = rstd_batch(var_s, rstd3, NT)
        W("vector", "AC", ac_r3)
        ywv_t7 = [0] * NT
        for t in range(NT):
            dv_a = OPI("vector", "DV", lambda E, t=t: E.tensor_scalar(
                y3n[:, 1024 * t:1024 * (t + 1)], stream[:, 1024 * t:1024 * (t + 1)],
                negm3[:, t:t + 1], rstd3[:, t:t + 1], ALU.add, ALU.mult))
            W("sync", "DV", dv_a)
            ywv_t7[t] = DMA("sync", "SQ", yb_d[128 * t:128 * (t + 1), :], y3n[:, 1024 * t:1024 * (t + 1)])

        pe_sp7 = 0
        dv_ra = 0
        ac_ebt = 0
        pe_t7 = 0
        dv_sc = 0
        pl_x3 = 0
        pl_x3_t = [0] * NT
        for sp in range(2):
            lo = 0 if sp == 0 else 1024
            span = 1408 if sp == 0 else 1280
            W("sync", "SQ", ywv_t7[10 if sp == 0 else 17])
            W("sync", "PE", pe_sp7)
            for cc in range(8):
                tv_ = DMA("sync", "SQ", y3nT[:, 1408 * cc:1408 * cc + span],
                          yb_d[lo:lo + span, 128 * cc:128 * (cc + 1)], transpose=True)
            W("tensor", "SQ", tv_)
            if sp == 0:
                W("tensor", "SG", sg_wb)
            ac_hist = [0, 0]
            ac_a = 0
            nchunks = [(0, 512), (512, 512), (1024, span - 1024)]
            for ee in range(8):
                for ci, (n0, nw) in enumerate(nchunks):
                    slot = (ee * 3 + ci) % 2
                    pslc = psB[:, 0:nw] if slot == 0 else psA[:, 0:nw]
                    W("tensor", "AC", ac_hist[slot])
                    def amm(E, ee=ee, n0=n0, nw=nw, pslc=pslc):
                        for cc in range(8):
                            r = E.matmul(pslc, Wb_sb[:, cc, 128 * ee:128 * ee + 128],
                                         y3nT[:, 1408 * cc + n0:1408 * cc + n0 + nw],
                                         start=(cc == 0), stop=(cc == 7))
                        return r
                    pe_a = OPI("tensor", "PE", amm)
                    W("scalar", "PE", pe_a)
                    ac_a = OPI("scalar", "AC", lambda E, ee=ee, n0=n0, nw=nw, pslc=pslc: E.copy(
                        AT_sb[:, 1408 * ee + n0:1408 * ee + n0 + nw], pslc))
                    ac_hist[slot] = ac_a
            if sp == 0:
                W("tensor", "SQ", bmv)
            qts = range(0, 9) if sp == 0 else range(9, 18)
            for t in qts:
                w0 = 0 if t == 0 else (1920 if t == 17 else 128 * (t - 1))
                mv = 0 if t == 0 else (2 if t == 17 else 1)
                W("tensor", "DV", dv_sc)
                W("tensor", "AC", ac_a)
                def smm7(E, t=t, w0=w0, lo=lo):
                    for cc in range(8):
                        r = E.matmul(psB[:, 0:384],
                                     AT_sb[:, 1408 * cc + 128 * t - lo:1408 * cc + 128 * t - lo + 128],
                                     y3nT[:, 1408 * cc + w0 - lo:1408 * cc + w0 - lo + 384],
                                     start=(cc == 0), stop=(cc == 7))
                    return r
                pe_s7 = OPI("tensor", "PE", smm7)
                W("vector", "PE", pe_s7)
                OPIW("vector", "DV", lambda E, mv=mv: E.tensor_add(
                    sc_sb[:, :], psB[:, 0:384], bmask[:, mv, :]))
                dv_sc = OPI("vector", "DV", lambda E: E.reduce_max(
                    negb[:, :], sc_sb[:, :], axis=AXX, negate=True))
                W("scalar", "DV", dv_sc)
                W("scalar", "PE", pe_t7)
                ac_e7 = OPI("scalar", "AC", lambda E: E.activation(
                    etri[:, 0:384], sc_sb[:, :], AF.Exp, bias=negb[:, :], scale=1.0,
                    accum_out=rcpb[:, :]))
                W("vector", "AC", ac_e7)
                dv_rc = OPI("vector", "DV", lambda E: E.reciprocal(rcpb[:, :], rcpb[:, :]))
                W("tensor", "AC", ac_e7)
                W("tensor", "AC", ac_ebt)
                def tmm(E):
                    for kbl in range(3):
                        r = E.transpose(psE[:, 128 * kbl:128 * (kbl + 1)],
                                        etri[:, 128 * kbl:128 * (kbl + 1)], ident[:, :])
                    return r
                pe_t7 = OPI("tensor", "PE", tmm)
                W("scalar", "PE", pe_t7)
                ac_ebt = OPI("scalar", "AC", lambda E: E.copy(EbT[:, 0:384], psE[:, 0:384]))
                W("tensor", "AC", ac_ebt)
                W("tensor", "DV", dv_ra)
                def pv7(E, w0=w0):
                    kt0 = w0 // 128
                    for kbl in range(3):
                        for k in range(2):
                            r = E.matmul(psD[:, 512 * k:512 * (k + 1)],
                                         EbT[:, 128 * kbl:128 * kbl + 128],
                                         y3n[:, 1024 * (kt0 + kbl) + 512 * k:1024 * (kt0 + kbl) + 512 * (k + 1)],
                                         start=(kbl == 0), stop=(kbl == 2))
                    return r
                pe_pv7 = OPI("tensor", "PE", pv7)
                W("vector", "PE", pe_pv7)
                W("vector", "DV", dv_rc)
                W("vector", "PL", pl_x3)
                def rafin(E, t=t):
                    E.tensor_scalar(ra_sb[:, 0:512], psD[:, 0:512], rcpb[:, :], 0.0, ALU.mult, ALU.max)
                    E.tensor_scalar(ra_sb[:, 512:1024], psD[:, 512:1024], rcpb[:, :], 0.0, ALU.mult, ALU.max)
                    return E.tensor_scalar(y3f[:, :], stream[:, 1024 * t:1024 * (t + 1)],
                                           negm3[:, t:t + 1], rstd3[:, t:t + 1], ALU.add, ALU.mult)
                dv_ra = OPI("vector", "DV", rafin)
                W("gpsimd", "DV", dv_ra)
                W("gpsimd", "PL", pl_x3)
                def x3add(E, t=t):
                    E.tensor_add(stream[:, 1024 * t:1024 * (t + 1)],
                                 stream[:, 1024 * t:1024 * (t + 1)], y3f[:, :])
                    return E.tensor_add(stream[:, 1024 * t:1024 * (t + 1)],
                                        stream[:, 1024 * t:1024 * (t + 1)], ra_sb[:, :])
                pl_x3 = OPI("gpsimd", "PL", x3add)
                pl_x3_t[t] = pl_x3
            pe_sp7 = sch.cnt["PE"]

        if "x3" in dump:
            dump_stream("x3", "PL", pl_x3)

        es7.close()
        with contextlib.ExitStack() as es8:
            pl_x4_t = ffn_phase("PL", pl_x3_t, es8)

        ov = 0
        for t in range(NT):
            W("sync", "PL", pl_x4_t[t])
            ov = DMA("sync", "SQ", out[128 * t:128 * (t + 1), :], stream[:, 1024 * t:1024 * (t + 1)])
        W("sync", "SQ", ov)

        @blockctx.tensor
        def _(E):
            for fn in sch.ops["tensor"]:
                fn(E)

        @blockctx.scalar
        def _(E):
            for fn in sch.ops["scalar"]:
                fn(E)

        @blockctx.vector
        def _(E):
            for fn in sch.ops["vector"]:
                fn(E)

        @blockctx.gpsimd
        def _(E):
            for fn in sch.ops["gpsimd"]:
                fn(E)

        @blockctx.sync
        def _(E):
            for fn in sch.ops["sync"]:
                fn(E)

    return nc


def _rope_tables():
    inv = (1.0 / (np.float32(10000.0) ** (np.arange(0, 64, 2, dtype=np.float32) / np.float32(64.0)))).astype(np.float32)
    ang = np.arange(S, dtype=np.float32)[:, None] * inv[None, :]
    cos = np.cos(ang).astype(np.float32)
    sin = np.sin(ang).astype(np.float32)
    cosT = np.empty((128, S), np.float32)
    sgnT = np.empty((128, S), np.float32)
    for r in range(128):
        j = r % 32
        cosT[r] = cos[:, j]
        sgnT[r] = (-sin[:, j]) if (r % 64) < 32 else sin[:, j]
    bf = ml_dtypes.bfloat16
    return np.ascontiguousarray(cosT.astype(bf)), np.ascontiguousarray(sgnT.astype(bf))


def _perm_rows(h0):
    n, r = [], []
    for h in range(h0, h0 + 4):
        ev = [64 * h + 2 * j for j in range(32)]
        od = [64 * h + 2 * j + 1 for j in range(32)]
        n += ev + od
        r += od + ev
    return np.array(n), np.array(r)


def _bmasks():
    m = np.zeros((3, 128, 384), np.float32)
    for v, d0 in ((0, 0), (1, 128), (2, 256)):
        i = np.arange(128)[:, None]
        j = np.arange(384)[None, :]
        delta = d0 + i - j
        m[v] = np.where(np.abs(delta) <= 64, 0.0, -1e9)
    return np.ascontiguousarray(m.astype(ml_dtypes.bfloat16))


def kernel(L0_attn_a_x, L0_attn_a_wq, L0_attn_a_wk, L0_attn_a_wv, L0_attn_a_wo,
           L0_attn_b_W, L0_ffn_W_r, L0_ffn_W_k, L0_ffn_W_v, L0_ffn_mu_r, L0_ffn_mu_k):
    x = np.ascontiguousarray(np.asarray(L0_attn_a_x, dtype=np.float32))
    wq = np.asarray(L0_attn_a_wq, np.float32)
    wk = np.asarray(L0_attn_a_wk, np.float32)
    wv = np.asarray(L0_attn_a_wv, np.float32)
    wo = np.asarray(L0_attn_a_wo, np.float32)
    Wb = np.asarray(L0_attn_b_W, np.float32)
    Fr = np.asarray(L0_ffn_W_r, np.float32)
    Fk = np.asarray(L0_ffn_W_k, np.float32)
    Fv = np.asarray(L0_ffn_W_v, np.float32)
    mu_r = np.asarray(L0_ffn_mu_r, np.float32)
    mu_k = np.asarray(L0_ffn_mu_k, np.float32)

    bf = ml_dtypes.bfloat16
    cosT, sgnT = _rope_tables()
    tri = np.ascontiguousarray(
        (np.arange(128)[:, None] <= np.arange(128)[None, :]).astype(np.float32).astype(bf))
    bm = _bmasks()
    ident = np.ascontiguousarray(np.eye(128, dtype=np.float32).astype(bf))
    woT = np.ascontiguousarray(wo.T)
    FrT = np.ascontiguousarray(Fr.T)
    FkT = np.ascontiguousarray(Fk.T)
    FvT = np.ascontiguousarray(Fv.T)
    mu_r2 = np.ascontiguousarray(mu_r.reshape(8, 128).T)
    mu_k2 = np.ascontiguousarray(mu_k.reshape(8, 128).T)

    in_maps = []
    for c in range(8):
        b, g = c // 4, c % 4
        off = OFFS[g]
        rn, rr = _perm_rows(4 * g)
        in_maps.append({
            "x_full": x[b],
            "x_slice": np.ascontiguousarray(x[b, off:off + WIN]),
            "wqT_n": np.ascontiguousarray(wq[rn].T),
            "wqT_r": np.ascontiguousarray(wq[rr].T),
            "wkT_n": np.ascontiguousarray(wk[rn].T),
            "wkT_r": np.ascontiguousarray(wk[rr].T),
            "wvT": np.ascontiguousarray(wv[4 * g * 64:(4 * g + 4) * 64].T),
            "woT": woT,
            "FrT": FrT, "FkT": FkT, "FvT": FvT,
            "Wb": Wb,
            "mu_r": mu_r2, "mu_k": mu_k2,
            "rope_cos": cosT, "rope_sgn": sgnT,
            "tri": tri, "bmask": bm, "ident": ident,
            "goff": np.array([[off]], dtype=np.int32),
        })

    nc = build_program()
    res = run_bass_kernel_spmd(nc, in_maps, core_ids=list(range(8)))
    global _last_res
    _last_res = res

    outp = np.empty((2, S, DM), np.float32)
    for c in range(8):
        b, g = c // 4, c % 4
        o = np.asarray(res.results[c]["out"], np.float32)
        outp[b, 2048 * g:2048 * (g + 1)] = o[R0[g]:R0[g] + 2048]
    return outp


# version 16
# speedup vs baseline: 1.0348x; 1.0348x over previous
import contextlib
import dataclasses
import numpy as np
import ml_dtypes
import concourse.bass as bass
import concourse.mybir as mybir
from concourse.bass_utils import run_bass_kernel_spmd

F32 = mybir.dt.float32
BF16 = mybir.dt.bfloat16
I32 = mybir.dt.int32
AF = mybir.ActivationFunctionType
ALU = mybir.AluOpType
AXX = mybir.AxisListType.X

S = 8192
DM = 1024
WIN = 2304
NT = 18
NH = 4
OFFS = [0, 1920, 3968, 5888]
R0 = [0, 128, 128, 256]

ENGINES = ("tensor", "scalar", "vector", "gpsimd", "sync")


class Sched:
    def __init__(self):
        self.ops = {e: [] for e in ENGINES}
        self.cnt = {}

    def emit(self, eng, fn):
        self.ops[eng].append(fn)

    def inc(self, sem, k=1):
        self.cnt[sem] = self.cnt.get(sem, 0) + k
        return self.cnt[sem]


def build_program(dump=()):
    nc = bass.Bass("TRN2", target_bir_lowering=False)

    def din(name, shape, dt=F32):
        return nc.dram_tensor(name, shape, dt, kind="ExternalInput").ap()

    x_full = din("x_full", [S, DM])
    x_slice = din("x_slice", [WIN, DM])
    wqT_n = din("wqT_n", [DM, 256]); wqT_r = din("wqT_r", [DM, 256])
    wkT_n = din("wkT_n", [DM, 256]); wkT_r = din("wkT_r", [DM, 256])
    wvT = din("wvT", [DM, 256])
    woT = din("woT", [DM, DM])
    FrT = din("FrT", [DM, DM]); FkT = din("FkT", [DM, DM]); FvT = din("FvT", [DM, DM])
    WbI = din("Wb", [DM, DM])
    mu_r_in = din("mu_r", [128, 8]); mu_k_in = din("mu_k", [128, 8])
    rope_cos = din("rope_cos", [128, S], BF16)
    rope_sgn = din("rope_sgn", [128, S], BF16)
    tri_in = din("tri", [128, 128], BF16)
    bmask_in = din("bmask", [3, 128, 384], BF16)
    ident_in = din("ident", [128, 128], BF16)
    goff = din("goff", [1, 1], I32)

    out = nc.dram_tensor("out", [WIN, DM], F32, kind="ExternalOutput").ap()

    dumps = {}
    def dout(name, shape, dt):
        dumps[name] = nc.dram_tensor("d_" + name, shape, dt, kind="ExternalOutput").ap()
        return dumps[name]

    y1n_d = nc.dram_tensor("y1n_d", [S, DM], BF16).ap()
    yb_d = nc.dram_tensor("yb_d", [WIN, DM], BF16).ap()
    ag_in = nc.dram_tensor("ag_in", [256, S], BF16).ap()
    ag1_out = nc.dram_tensor("ag1_out", [512, S], BF16).ap()
    ag2_out = nc.dram_tensor("ag2_out", [512, S], BF16).ap()

    sch = Sched()
    es = contextlib.ExitStack()
    sems = {}

    def W(eng, s, v):
        if v <= 0:
            return
        sch.emit(eng, lambda E, h=sems[s], v=v: E.wait_ge(h, v))

    def DMA(eng, s, o, i, **kw):
        v = sch.inc(s, 16)
        sch.emit(eng, lambda E, o=o, i=i, h=sems[s], kw=kw: E.dma_start(out=o, in_=i, **kw).then_inc(h, 16))
        return v

    def OP(eng, fn):
        sch.emit(eng, fn)

    def OPI(eng, s, fn):
        v = sch.inc(s, 1)
        sch.emit(eng, lambda E, fn=fn, h=sems[s]: fn(E).then_inc(h, 1))
        return v

    def OPIW(eng, s, fn):
        v = OPI(eng, s, fn)
        W(eng, s, v)
        return v

    with es:
        for sname in ("SQ", "SG", "PE", "AC", "DV", "PL", "CCA", "CCB"):
            sems[sname] = es.enter_context(nc.semaphore(sname))

        def sb(name, shape, dt):
            return es.enter_context(nc.sbuf_tensor(name, shape, dt))

        tri = sb("tri_sb", [128, 128], BF16)
        ident = sb("ident_sb", [128, 128], BF16)
        ones1 = sb("ones1", [1, 128], F32)
        bias3 = sb("bias3", [128, 1], F32)
        eps_t = sb("eps_t", [128, 1], F32)
        mu_r = sb("mu_r_sb", [128, 8], F32); mu_k = sb("mu_k_sb", [128, 8], F32)
        om_r = sb("om_r", [128, 8], F32); om_k = sb("om_k", [128, 8], F32)
        off_sb = sb("off_sb", [1, 1], I32)
        negm3 = sb("negm3", [128, NT], F32)
        rstd3 = sb("rstd3", [128, NT], F32)
        negm_s = sb("negm_s", [128, NT], F32)
        var_s = sb("var_s", [128, NT], F32)
        rstd_s = sb("rstd_s", [128, NT], F32)
        bnst = sb("bnst", [128, 12], F32)
        bnsc = sb("bnsc", [128, 8], F32)
        etri = sb("etri", [128, 512], BF16)

        es_a = contextlib.ExitStack()
        def sba(name, shape, dt):
            return es_a.enter_context(nc.sbuf_tensor(name, shape, dt))
        wq_n = sba("wq_n", [128, 8, 256], BF16)
        wq_r = sba("wq_r", [128, 8, 256], BF16)
        wk_n = sba("wk_n", [128, 8, 256], BF16)
        wk_r = sba("wk_r", [128, 8, 256], BF16)
        wv_s = sba("wv_s", [128, 8, 256], BF16)
        QT = [sba(f"QT{g}", [128, S], BF16) for g in range(2)]
        KT = [sba(f"KT{g}", [128, S], BF16) for g in range(2)]
        Vaug = [sba(f"Vaug{h}", [128, 64 * 65], BF16) for h in range(NH)]

        psA = es.enter_context(nc.psum_tensor("psA", [128, 2048], F32))
        psB = es.enter_context(nc.psum_tensor("psB", [128, 512], F32))
        psD = es.enter_context(nc.psum_tensor("psD", [128, 1024], F32))
        psE = es.enter_context(nc.psum_tensor("psE", [128, 512], BF16))

        blockctx = es.enter_context(nc.Block())

        for dst, src in ((wq_n, wqT_n), (wq_r, wqT_r), (wk_n, wkT_n), (wk_r, wkT_r), (wv_s, wvT)):
            sg_w0 = DMA("gpsimd", "SG", dst[:, :, :], src.rearrange("(c p) d -> p c d", p=128))
        DMA("sync", "SQ", tri[:, :], tri_in)
        DMA("sync", "SQ", ident[:, :], ident_in)
        DMA("sync", "SQ", mu_r[:, :], mu_r_in)
        DMA("sync", "SQ", mu_k[:, :], mu_k_in)
        sq_c0 = DMA("sync", "SQ", off_sb[:, :], goff)

        def _memsets(E):
            E.memset(ones1[:, :], 1.0)
            E.memset(ones1b[:, :], 1.0)
            E.memset(bias3[:, :], -3.0)
            for h in range(NH):
                E.memset(Vaug[h][:, :], 1.0)
            return E.memset(eps_t[:, :], 1e-5)
        OPI("vector", "DV", _memsets)
        W("vector", "SQ", sq_c0)
        OP("vector", lambda E: E.tensor_scalar(om_r[:, :], mu_r[:, :], -1.0, 1.0, ALU.mult, ALU.add))
        dv_c0 = OPI("vector", "DV", lambda E: E.tensor_scalar(om_k[:, :], mu_k[:, :], -1.0, 1.0, ALU.mult, ALU.add))

        def ln_stats(xap, col, negm_t, var_t):
            def s1(E, xap=xap):
                E.bn_stats(bnst[:, 0:6], xap[:, 0:512])
                return E.bn_stats(bnst[:, 6:12], xap[:, 512:1024])
            OPIW("vector", "DV", s1)
            def s2(E, col=col, negm_t=negm_t, var_t=var_t):
                E.reduce_sum(negm_t[:, col:col + 1], bnst[:, 1:12:3], axis=AXX, negate=True)
                E.reduce_sum(var_t[:, col:col + 1], bnst[:, 2:12:3], axis=AXX)
                return E.tensor_mul(bnsc[:, 1:4], bnst[:, 4:12:3], bnst[:, 4:12:3])
            OPIW("vector", "DV", s2)
            def s3(E, col=col, negm_t=negm_t, var_t=var_t):
                E.reduce_sum(bnsc[:, 4:5], bnsc[:, 1:4], axis=AXX)
                E.tensor_mul(bnsc[:, 5:6], bnst[:, 1:2], bnst[:, 1:2])
                E.tensor_mul(bnsc[:, 6:7], negm_t[:, col:col + 1], negm_t[:, col:col + 1])
                return E.tensor_scalar_mul(var_t[:, col:col + 1], var_t[:, col:col + 1], 1.0 / 1024.0)
            OPIW("vector", "DV", s3)
            def s4(E, col=col, negm_t=negm_t):
                E.tensor_add(bnsc[:, 7:8], bnsc[:, 4:5], bnsc[:, 5:6])
                E.tensor_scalar_mul(negm_t[:, col:col + 1], negm_t[:, col:col + 1], 0.25)
                return E.tensor_scalar_mul(bnsc[:, 6:7], bnsc[:, 6:7], 1.0 / 16.0)
            OPIW("vector", "DV", s4)
            def s5(E, col=col, var_t=var_t):
                return E.scalar_tensor_tensor(bnsc[:, 7:8], bnsc[:, 7:8], 0.25,
                                              var_t[:, col:col + 1], ALU.mult, ALU.add)
            OPIW("vector", "DV", s5)
            def s6(E, col=col, var_t=var_t):
                return E.tensor_sub(var_t[:, col:col + 1], bnsc[:, 7:8], bnsc[:, 6:7])
            return OPI("vector", "DV", s6)

        def rstd_batch(var_t, rstd_t, n):
            def _f1(E, var_t=var_t, rstd_t=rstd_t, n=n):
                return E.activation(rstd_t[:, 0:n], var_t[:, 0:n], AF.Ln, bias=eps_t[:, :], scale=1.0)
            OPIW("scalar", "AC", _f1)
            def _f2(E, rstd_t=rstd_t, n=n):
                return E.activation(rstd_t[:, 0:n], rstd_t[:, 0:n], AF.Exp, bias=0.0, scale=-0.5)
            return OPI("scalar", "AC", _f2)

        rs_sb = [sba(f"rs{i}", [1, 512], F32) for i in range(4)]
        rsb_sb = [sba(f"rsb{i}", [1, 512], BF16) for i in range(4)]
        ones1b = sba("ones1b", [1, 64], BF16)
        ou_sb = [sba(f"ou{i}", [64, 512], F32) for i in range(4)]
        ostg = [sba(f"ostg{i}", [64, 512], BF16) for i in range(4)]
        Eb_s = [sba(f"Ebuf{i}", [128, 1024], BF16) for i in range(2)]

        pv_hist = [0, 0]
        exp_hist = [0, 0]
        msk_dv = [0, 0]
        psd_free = [dv_c0, 0]
        norm_dv = [dv_c0] * 4
        stg_dma = [0] * 4
        stg_last = [0, 0]
        kv_pl = 0
        kv_ac = 0
        deferred = []

        def emit_attn_batch(qbs):
            W("tensor", "PL", kv_pl)
            W("tensor", "AC", kv_ac)
            for g2 in range(2):
                for qb in qbs:
                    qs = 512 * qb
                    nkb = 4 * (qb + 1)
                    pend = None

                    def emit_pv(args, last, qb=qb):
                        (kb, reg, diag, col0, first, eb) = args
                        W("tensor", "AC", exp_hist[reg])
                        if diag:
                            W("tensor", "DV", msk_dv[reg])
                        if first:
                            W("tensor", "DV", psd_free[0])
                        def pvmm(E, g2=g2, kb=kb, diag=diag, col0=col0, eb=eb,
                                 first=first, last=last):
                            r = None
                            for hh in range(2):
                                hcol = 512 * hh
                                V = Vaug[2 * g2 + hh]
                                if diag:
                                    r = E.matmul(psD[0:65, hcol + col0:hcol + col0 + 128],
                                                 V[:, 65 * kb:65 * kb + 65],
                                                 etri[:, 256 * (kb % 2) + 128 * hh:256 * (kb % 2) + 128 * hh + 128],
                                                 start=first, stop=last)
                                    if col0 + 128 < 512:
                                        r = E.matmul(psD[0:65, hcol + col0 + 128:hcol + 512],
                                                     V[:, 65 * kb:65 * kb + 65],
                                                     eb[:, hcol + col0 + 128:hcol + 512],
                                                     start=first, stop=False)
                                else:
                                    r = E.matmul(psD[0:65, hcol:hcol + 512],
                                                 V[:, 65 * kb:65 * kb + 65],
                                                 eb[:, hcol:hcol + 512],
                                                 start=first, stop=False)
                            return r
                        v = OPI("tensor", "PE", pvmm)
                        pv_hist[reg] = v
                        return v

                    for kb in range(nkb):
                        reg = kb % 2
                        r0c = 1024 * reg
                        diag = kb >= 4 * qb
                        col0 = 128 * (kb - 4 * qb) if diag else 0
                        W("tensor", "AC", exp_hist[reg])
                        def smm(E, g2=g2, kb=kb, qs=qs, col0=col0, r0c=r0c):
                            E.matmul(psA[:, r0c + col0:r0c + 512],
                                     KT[g2][0:64, 128 * kb:128 * kb + 128],
                                     QT[g2][0:64, qs + col0:qs + 512],
                                     start=True, stop=True)
                            return E.matmul(psA[:, r0c + 512 + col0:r0c + 1024],
                                            KT[g2][64:128, 128 * kb:128 * kb + 128],
                                            QT[g2][64:128, qs + col0:qs + 512],
                                            start=True, stop=True)
                        pe_s = OPI("tensor", "PE", smm)
                        if kb == 1 and deferred:
                            for fn in deferred:
                                fn()
                            deferred.clear()
                        W("scalar", "PE", pe_s)
                        W("scalar", "PE", pv_hist[reg])
                        eb = Eb_s[reg]
                        ac_e = OPI("scalar", "AC", lambda E, eb=eb, r0c=r0c: E.activation(
                            eb[:, 0:1024], psA[:, r0c:r0c + 1024], AF.Exp,
                            bias=bias3[:, :], scale=0.125))
                        exp_hist[reg] = ac_e
                        if diag:
                            W("vector", "AC", ac_e)
                            W("vector", "PE", pv_hist[reg])
                            def msk(E, eb=eb, col0=col0, kb=kb):
                                E.tensor_mul(etri[:, 256 * (kb % 2):256 * (kb % 2) + 128],
                                             eb[:, col0:col0 + 128], tri[:, :])
                                return E.tensor_mul(etri[:, 256 * (kb % 2) + 128:256 * (kb % 2) + 256],
                                                    eb[:, 512 + col0:512 + col0 + 128], tri[:, :])
                            msk_dv[reg] = OPI("vector", "DV", msk)
                        if pend is not None:
                            emit_pv(pend, last=False)
                        pend = (kb, reg, diag, col0, kb == 0, eb)
                    pv_last = emit_pv(pend, last=True)

                    dv_rs = [0, 0]
                    ac_os = [0, 0]
                    for hh in range(2):
                        slot = 2 * (qb % 2) + hh
                        hcol = 512 * hh
                        W("vector", "PE", pv_last)
                        W("vector", "DV", norm_dv[slot])
                        OPIW("vector", "DV", lambda E, slot=slot, hcol=hcol: E.tensor_copy(
                            rs_sb[slot][:, :], psD[64:65, hcol:hcol + 512]))
                        OPIW("vector", "DV", lambda E, slot=slot: E.reciprocal(
                            rs_sb[slot][:, :], rs_sb[slot][:, :]))
                        dv_rs[hh] = OPI("vector", "DV", lambda E, slot=slot: E.tensor_copy(
                            rsb_sb[slot][:, :], rs_sb[slot][:, :]))
                        ac_os[hh] = OPI("vector", "DV", lambda E, slot=slot, hcol=hcol: E.tensor_copy(
                            ou_sb[slot][:, :], psD[0:64, hcol:hcol + 512]))
                    psd_free[0] = sch.cnt["DV"]
                    psd_free[1] = 0

                    def mk_tail(qb=qb, g2=g2, qs=qs, dv_rs=tuple(dv_rs), ac_os=tuple(ac_os)):
                        def tailf():
                            for hh in range(2):
                                slot = 2 * (qb % 2) + hh
                                W("tensor", "DV", dv_rs[hh])
                                W("tensor", "DV", norm_dv[slot])
                                pe_bc = OPI("tensor", "PE", lambda E, slot=slot, hh=hh: E.matmul(
                                    psB[64 * hh:64 * hh + 64, :], ones1b[:, :],
                                    rsb_sb[slot][:, :], start=True, stop=True))
                                W("vector", "PE", pe_bc)
                                W("vector", "DV", ac_os[hh])
                                W("vector", "SQ", stg_dma[slot])
                                def norm(E, hh=hh, slot=slot):
                                    return E.tensor_mul(ostg[slot][:, :],
                                                        ou_sb[slot][:, :], psB[64 * hh:64 * hh + 64, :])
                                norm_dv[slot] = OPI("vector", "DV", norm)
                                W("sync", "DV", norm_dv[slot])
                                stg_dma[slot] = DMA("sync", "SQ",
                                                    ag_in[128 * g2 + 64 * hh:128 * g2 + 64 * hh + 64, qs:qs + 512],
                                                    ostg[slot][:, :])
                                stg_last[g2] = stg_dma[slot]
                        return tailf
                    deferred.append(mk_tail())

        with contextlib.ExitStack() as es12:
            xch = [es12.enter_context(nc.sbuf_tensor(f"xch{i}", [128, 1024], F32)) for i in range(2)]
            xap = [es12.enter_context(nc.sbuf_tensor(f"xap{i}", [128, 1024], F32)) for i in range(2)]
            ych = [es12.enter_context(nc.sbuf_tensor(f"ych{i}", [128, 1024], BF16)) for i in range(2)]
            y1nT = [es12.enter_context(nc.sbuf_tensor(f"y1nT{c}", [128, 1024], BF16)) for c in range(8)]
            cos_st = es12.enter_context(nc.sbuf_tensor("cos_st", [128, 1024], BF16))
            sgn_st = es12.enter_context(nc.sbuf_tensor("sgn_st", [128, 1024], BF16))
            rsc = [[es12.enter_context(nc.sbuf_tensor(f"rsc{p}_{i}", [128, 512], BF16))
                    for i in range(4)] for p in range(2)]
            rtmp1 = es12.enter_context(nc.sbuf_tensor("rtmp1", [128, 512], F32))
            rtmp2 = es12.enter_context(nc.sbuf_tensor("rtmp2", [128, 512], F32))
            bnq = es12.enter_context(nc.sbuf_tensor("bnq", [128, 16, 12], F32))
            msq = es12.enter_context(nc.sbuf_tensor("msq", [128, 16, 4], F32))
            s2q = es12.enter_context(nc.sbuf_tensor("s2q", [128, 16], F32))
            w1q = es12.enter_context(nc.sbuf_tensor("w1q", [128, 16], F32))
            uq = es12.enter_context(nc.sbuf_tensor("uq", [128, 16], F32))

            xch_free = [0, 0]
            xap_free = [0, 0]
            ych_free = [0, 0]
            last_y1w = 0
            rope_done = 0
            evac_hist = [0, 0]
            rope_pl = [0, 0]
            v_evac_h = [0, 0]
            pe_qkv_done = 0

            def emit_ln_quarter(qs_):
                nonlocal last_y1w
                st_dv = 0
                for ct in range(16):
                    gc = 16 * qs_ + ct
                    bi = gc % 2
                    W("sync", "DV", xch_free[bi])
                    xv = DMA("sync", "SQ", xch[bi][:, :], x_full[128 * gc:128 * (gc + 1), :])
                    W("vector", "SQ", xv)
                    def st(E, bi=bi, ct=ct):
                        E.bn_stats(bnq[:, ct, 0:6], xch[bi][:, 0:512])
                        return E.bn_stats(bnq[:, ct, 6:12], xch[bi][:, 512:1024])
                    st_dv = OPI("vector", "DV", st)
                    xch_free[bi] = st_dv
                W("vector", "DV", st_dv)
                W("vector", "PL", max(xap_free))
                def cb1(E):
                    for t in range(16):
                        E.reduce_sum(negm_s[:, t:t + 1], bnq[:, t, 1:12:3], axis=AXX, negate=True)
                        E.reduce_sum(var_s[:, t:t + 1], bnq[:, t, 2:12:3], axis=AXX)
                        r = E.tensor_mul(msq[:, t, :], bnq[:, t, 1:12:3], bnq[:, t, 1:12:3])
                    return r
                OPIW("vector", "DV", cb1)
                def cb2(E):
                    for t in range(16):
                        r = E.reduce_sum(s2q[:, t:t + 1], msq[:, t, :], axis=AXX)
                    E.tensor_mul(uq[:, :], negm_s[:, 0:16], negm_s[:, 0:16])
                    return E.tensor_scalar_mul(w1q[:, :], var_s[:, 0:16], 1.0 / 1024.0)
                OPIW("vector", "DV", cb2)
                def cb3(E):
                    E.tensor_scalar_mul(negm_s[:, 0:16], negm_s[:, 0:16], 0.25)
                    return E.scalar_tensor_tensor(s2q[:, :], s2q[:, :], 0.25, w1q[:, :], ALU.mult, ALU.add)
                OPIW("vector", "DV", cb3)
                def cb4(E):
                    return E.scalar_tensor_tensor(var_s[:, 0:16], uq[:, :], -1.0 / 16.0,
                                                  s2q[:, :], ALU.mult, ALU.add)
                cb_dv = OPI("vector", "DV", cb4)
                W("scalar", "DV", cb_dv)
                ac_r = rstd_batch(var_s, rstd_s, 16)
                W("gpsimd", "DV", cb_dv)
                W("gpsimd", "AC", ac_r)
                for ct in range(16):
                    gc = 16 * qs_ + ct
                    bi = gc % 2
                    W("sync", "PL", xap_free[bi])
                    xv = DMA("sync", "SQ", xap[bi][:, :], x_full[128 * gc:128 * (gc + 1), :])
                    W("gpsimd", "SQ", xv)
                    W("gpsimd", "SQ", ych_free[bi])
                    pl_ap = OPI("gpsimd", "PL", lambda E, bi=bi, ct=ct: E.tensor_scalar(
                        ych[bi][:, :], xap[bi][:, :],
                        negm_s[:, ct:ct + 1], rstd_s[:, ct:ct + 1], ALU.add, ALU.mult))
                    xap_free[bi] = pl_ap
                    W("sync", "PL", pl_ap)
                    yv = DMA("sync", "SQ", y1n_d[128 * gc:128 * (gc + 1), :], ych[bi][:, :])
                    ych_free[bi] = yv
                    last_y1w = yv

            emit_ln_quarter(0)
            for qt in range(4):
                for half in range(2):
                    eh = 2 * qt + half
                    W("sync", "SQ", last_y1w)
                    W("sync", "PE", pe_qkv_done)
                    for cc in range(8):
                        tv = DMA("sync", "SQ", y1nT[cc][:, :],
                                 y1n_d[1024 * eh:1024 * (eh + 1), 128 * cc:128 * (cc + 1)], transpose=True)
                    W("sync", "PL", rope_done)
                    DMA("sync", "SQ", cos_st[:, :], rope_cos[:, 1024 * eh:1024 * (eh + 1)])
                    tab_done = DMA("sync", "SQ", sgn_st[:, :], rope_sgn[:, 1024 * eh:1024 * (eh + 1)])

                    W("tensor", "SQ", tab_done)
                    if qt == 0 and half == 0:
                        W("tensor", "SG", sg_w0)
                    W("tensor", "AC", max(exp_hist))
                    for g2 in range(2):
                        for nb in range(2):
                            par = nb % 2
                            W("tensor", "AC", evac_hist[0])
                            def qmm(E, g2=g2, nb=nb):
                                for sl, w in ((0, wq_n), (1, wq_r)):
                                    for cc in range(8):
                                        r = E.matmul(psA[:, 512 * sl:512 * (sl + 1)],
                                                     w[:, cc, 128 * g2:128 * g2 + 128],
                                                     y1nT[cc][:, 512 * nb:512 * nb + 512],
                                                     start=(cc == 0), stop=(cc == 7))
                                return r
                            pe_q = OPI("tensor", "PE", qmm)
                            W("tensor", "AC", evac_hist[1])
                            def kmm(E, g2=g2, nb=nb):
                                for sl, w in ((2, wk_n), (3, wk_r)):
                                    for cc in range(8):
                                        r = E.matmul(psA[:, 512 * sl:512 * (sl + 1)],
                                                     w[:, cc, 128 * g2:128 * g2 + 128],
                                                     y1nT[cc][:, 512 * nb:512 * nb + 512],
                                                     start=(cc == 0), stop=(cc == 7))
                                return r
                            pe_k = OPI("tensor", "PE", kmm)
                            W("scalar", "PE", pe_q)
                            W("scalar", "PL", rope_pl[par])
                            def evacq(E, par=par):
                                E.copy(rsc[par][0][:, :], psA[:, 0:512])
                                return E.copy(rsc[par][1][:, :], psA[:, 512:1024])
                            evac_hist[0] = OPI("scalar", "AC", evacq)
                            W("scalar", "PE", pe_k)
                            def evack(E, par=par):
                                E.copy(rsc[par][2][:, :], psA[:, 1024:1536])
                                return E.copy(rsc[par][3][:, :], psA[:, 1536:2048])
                            ac_ev = OPI("scalar", "AC", evack)
                            evac_hist[1] = ac_ev
                            col = 1024 * eh + 512 * nb
                            W("gpsimd", "AC", ac_ev)
                            def ropem(E, par=par, nb=nb):
                                E.tensor_mul(rtmp1[:, :], rsc[par][0][:, :], cos_st[:, 512 * nb:512 * nb + 512])
                                return E.tensor_mul(rtmp2[:, :], rsc[par][1][:, :], sgn_st[:, 512 * nb:512 * nb + 512])
                            v = OPI("gpsimd", "PL", ropem)
                            W("gpsimd", "PL", v)
                            OPI("gpsimd", "PL", lambda E, g2=g2, col=col: E.tensor_add(
                                QT[g2][:, col:col + 512], rtmp1[:, :], rtmp2[:, :]))
                            def ropem2(E, par=par, nb=nb):
                                E.tensor_mul(rtmp1[:, :], rsc[par][2][:, :], cos_st[:, 512 * nb:512 * nb + 512])
                                return E.tensor_mul(rtmp2[:, :], rsc[par][3][:, :], sgn_st[:, 512 * nb:512 * nb + 512])
                            v = OPI("gpsimd", "PL", ropem2)
                            W("gpsimd", "PL", v)
                            rope_pl[par] = OPI("gpsimd", "PL", lambda E, g2=g2, col=col: E.tensor_add(
                                KT[g2][:, col:col + 512], rtmp1[:, :], rtmp2[:, :]))
                    rope_done = max(rope_pl)

                    W("tensor", "AC", max(exp_hist))
                    for nb in range(2):
                        for tt in range(4):
                            kb = 8 * eh + 4 * nb + tt
                            vslc = psA[:, 0:256] if kb % 2 == 0 else psA[:, 512:768]
                            W("tensor", "AC", v_evac_h[kb % 2])
                            def vmm(E, nb=nb, tt=tt, vslc=vslc):
                                for cc in range(8):
                                    r = E.matmul(vslc,
                                                 y1nT[cc][:, 512 * nb + 128 * tt:512 * nb + 128 * tt + 128],
                                                 wv_s[:, cc, :], start=(cc == 0), stop=(cc == 7))
                                return r
                            pe_v = OPI("tensor", "PE", vmm)
                            W("scalar", "PE", pe_v)
                            def vev(E, kb=kb, vslc=vslc):
                                for h in range(NH):
                                    r = E.copy(Vaug[h][:, 65 * kb:65 * kb + 64],
                                               vslc[:, 64 * h:64 * h + 64])
                                return r
                            v_evac_h[kb % 2] = OPI("scalar", "AC", vev)
                    pe_qkv_done = sch.cnt["PE"]
                    if half == 0 and qt < 3:
                        emit_ln_quarter(qt + 1)
                    kv_pl = rope_done
                    kv_ac = max(v_evac_h)
                    emit_attn_batch(range(2 * eh, 2 * eh + 2))

            for fn in deferred:
                fn()
            deferred.clear()

            if "qkv" in dump:
                dqt = dout("QT", [2, 128, S], BF16)
                dkt = dout("KT", [2, 128, S], BF16)
                dva = dout("Vaug", [NH, 128, 64 * 65], BF16)
                W("sync", "PL", rope_done)
                W("sync", "AC", max(v_evac_h))
                for g in range(2):
                    DMA("sync", "SQ", dqt[g], QT[g][:, :])
                    DMA("sync", "SQ", dkt[g], KT[g][:, :])
                for h in range(NH):
                    DMA("sync", "SQ", dva[h], Vaug[h][:, :])
            if "y1" in dump:
                dy1 = dout("y1n", [S, DM], BF16)
                W("sync", "SQ", last_y1w)
                DMA("sync", "SQ", dy1, y1n_d)

        W("gpsimd", "SQ", stg_last[0])
        OPI("gpsimd", "CCA", lambda E: E.collective_compute(
            "AllGather", ALU.bypass,
            replica_groups=[[0, 1, 2, 3], [4, 5, 6, 7]],
            ins=[ag_in[0:128, :].opt()], outs=[ag1_out.opt()]))
        W("gpsimd", "SQ", stg_last[1])
        OPI("gpsimd", "CCB", lambda E: E.collective_compute(
            "AllGather", ALU.bypass,
            replica_groups=[[0, 1, 2, 3], [4, 5, 6, 7]],
            ins=[ag_in[128:256, :].opt()], outs=[ag2_out.opt()]))
        agv = max(stg_dma)

        if "attn" in dump:
            dat = dout("ag_in", [256, S], BF16)
            W("sync", "SQ", agv)
            DMA("sync", "SQ", dat, ag_in)

        es_a.close()
        stream = sb("stream", [128, NT * 1024], F32)
        es5 = contextlib.ExitStack()
        G_sb = [es5.enter_context(nc.sbuf_tensor(f"G{cc}", [128, WIN], BF16)) for cc in range(8)]
        wo_sb = es5.enter_context(nc.sbuf_tensor("wo_sb", [128, 8, 1024], BF16))
        xsl = [es5.enter_context(nc.sbuf_tensor(f"xsl{i}", [128, 1024], F32)) for i in range(2)]

        sg_wo = DMA("gpsimd", "SG", wo_sb[:, :, :], woT.rearrange("(c p) d -> p c d", p=128))

        W("sync", "SQ", sq_c0)
        W("sync", "CCA", 1)
        vG = sch.inc("SQ", 8 * 16)
        def gload_even(E):
            t0 = E.value_load(off_sb[0:1, 0:1])
            for cc in range(0, 8, 2):
                base = ag1_out[128 * (cc // 2):128 * (cc // 2) + 128, 0:WIN]
                dyn = dataclasses.replace(base, offset=base.offset + t0)
                r = E.dma_start(out=G_sb[cc][:, :], in_=dyn).then_inc(sems["SQ"], 16)
            return r
        sch.emit("sync", gload_even)
        W("sync", "CCB", 1)
        def gload_odd(E):
            t0 = E.value_load(off_sb[0:1, 0:1])
            for cc in range(1, 8, 2):
                base = ag2_out[128 * (cc // 2):128 * (cc // 2) + 128, 0:WIN]
                dyn = dataclasses.replace(base, offset=base.offset + t0)
                r = E.dma_start(out=G_sb[cc][:, :], in_=dyn).then_inc(sems["SQ"], 16)
            return r
        sch.emit("sync", gload_odd)

        x1_dv = [dv_c0, dv_c0]
        x1_dv_t = [0] * NT
        for t in range(NT):
            bi = t % 2
            W("sync", "DV", x1_dv[bi])
            xv = DMA("sync", "SQ", xsl[bi][:, :], x_slice[128 * t:128 * (t + 1), :])
            if t == 0:
                W("tensor", "SQ", vG)
                W("tensor", "SG", sg_wo)
            W("tensor", "DV", x1_dv[(t + 1) % 2])
            def womm(E, t=t):
                for cc in range(8):
                    for k in range(2):
                        r = E.matmul(psD[:, 512 * k:512 * (k + 1)],
                                     G_sb[cc][:, 128 * t:128 * t + 128],
                                     wo_sb[:, cc, 512 * k:512 * (k + 1)],
                                     start=(cc == 0), stop=(cc == 7))
                return r
            pe_p5 = OPI("tensor", "PE", womm)
            W("vector", "PE", pe_p5)
            W("vector", "SQ", xv)
            def x1add(E, t=t, bi=bi):
                E.tensor_add(stream[:, 1024 * t:1024 * t + 512], psD[:, 0:512], xsl[bi][:, 0:512])
                return E.tensor_add(stream[:, 1024 * t + 512:1024 * (t + 1)],
                                    psD[:, 512:1024], xsl[bi][:, 512:1024])
            x1_dv[bi] = OPI("vector", "DV", x1add)
            x1_dv_t[t] = x1_dv[bi]

        def dump_stream(name, ready_sem, ready_val):
            dx = dout(name, [WIN, DM], F32)
            W("sync", ready_sem, ready_val)
            vv = DMA("sync", "SQ", dx.rearrange("(n p) c -> p n c", p=128),
                     stream[:, :].rearrange("p (n c) -> p n c", c=1024))
            W("vector", "SQ", vv)
            W("gpsimd", "SQ", vv)

        if "x1" in dump:
            dump_stream("x1", "DV", max(x1_dv))

        _ffn_id = [0]
        def ffn_phase(ready_sem, ready_vals, esf):
            _ffn_id[0] += 1
            _sfx = f"_f{_ffn_id[0]}"
            def sbf(name, shape, dt):
                return esf.enter_context(nc.sbuf_tensor(name + _sfx, shape, dt))
            ynT = sbf("ynT", [128, 8 * 1280], BF16)
            xmix = sbf("xmix", [128, 8 * 1152], BF16)
            kaT = sbf("kaT", [128, 8 * 1152], BF16)
            krl = sbf("krl", [128, 512], BF16)
            ytile = [sbf(f"ytile{i}", [128, 1024], BF16) for i in range(2)]
            rsig = sbf("rsig", [128, 1024], F32)
            mix_sc = sbf("mix_sc", [128, 1152], F32)
            tv_sb = sbf("tv_sb", [128, 1024], F32)
            Fr_sb = sbf("Fr_sb", [128, 8, 1024], BF16)
            Fk_sb = sbf("Fk_sb", [128, 8, 1024], BF16)
            Fv_sb = sbf("Fv_sb", [128, 8, 1024], BF16)

            W("gpsimd", ready_sem, max(ready_vals) if isinstance(ready_vals, (list, tuple)) else ready_vals)
            DMA("gpsimd", "SG", Fk_sb[:, :, :], FkT.rearrange("(c p) d -> p c d", p=128))
            sg_fkr = DMA("gpsimd", "SG", Fr_sb[:, :, :], FrT.rearrange("(c p) d -> p c d", p=128))
            sg_fv = DMA("gpsimd", "SG", Fv_sb[:, :, :], FvT.rearrange("(c p) d -> p c d", p=128))
            rv = ready_vals if isinstance(ready_vals, (list, tuple)) else [ready_vals] * NT
            for t in range(NT):
                W("vector", ready_sem, rv[t])
                dv_st = ln_stats(stream[:, 1024 * t:1024 * (t + 1)], t, negm_s, var_s)
            W("scalar", "DV", dv_st)
            ac_r = rstd_batch(var_s, rstd_s, NT)
            W("vector", "AC", ac_r)
            yt_free = [0, 0]
            ywv_t = [0] * NT
            for t in range(NT):
                bi = t % 2
                W("vector", "SQ", yt_free[bi])
                dv_a = OPI("vector", "DV", lambda E, t=t, bi=bi: E.tensor_scalar(
                    ytile[bi][:, :], stream[:, 1024 * t:1024 * (t + 1)],
                    negm_s[:, t:t + 1], rstd_s[:, t:t + 1], ALU.add, ALU.mult))
                W("sync", "DV", dv_a)
                ywv_t[t] = DMA("sync", "SQ", yb_d[128 * t:128 * (t + 1), :], ytile[bi][:, :])
                yt_free[bi] = ywv_t[t]

            last_pl = 0
            pl_t = [0] * NT
            pe_sp = 0
            ac_sig_prev = 0
            for sp in range(2):
                lo = 0 if sp == 0 else 1024
                span = 1152 if sp == 0 else 1280
                shoff = 0 if sp == 0 else 128
                W("sync", "SQ", ywv_t[8 if sp == 0 else 17])
                W("sync", "DV", sch.cnt["DV"])
                for cc in range(8):
                    tv_ = DMA("sync", "SQ", ynT[:, 1280 * cc:1280 * cc + span],
                              yb_d[lo:lo + span, 128 * cc:128 * (cc + 1)], transpose=True)
                W("vector", "SQ", tv_)
                W("vector", "PE", pe_sp)

                def mix_one(mu, om, shoff=shoff, sp=sp):
                    r = 0
                    for cc in range(8):
                        b = 1280 * cc
                        o = 1152 * cc
                        if sp == 0:
                            def _a(E, b=b, o=o, mu=mu, om=om):
                                E.tensor_scalar(xmix[:, o:o + 1], ynT[:, b:b + 1],
                                                mu[:, cc if False else 0:1], None, ALU.mult)
                                return E.tensor_scalar(mix_sc[:, 0:1151], ynT[:, b:b + 1151],
                                                       om[:, 0:1], None, ALU.mult)
                            def _a(E, b=b, o=o, mu=mu, om=om, cc=cc):
                                E.tensor_scalar(xmix[:, o:o + 1], ynT[:, b:b + 1],
                                                mu[:, cc:cc + 1], None, ALU.mult)
                                return E.tensor_scalar(mix_sc[:, 0:1151], ynT[:, b:b + 1151],
                                                       om[:, cc:cc + 1], None, ALU.mult)
                            OPIW("vector", "DV", _a)
                            def _b(E, b=b, o=o, mu=mu, cc=cc):
                                return E.scalar_tensor_tensor(xmix[:, o + 1:o + 1152],
                                                              ynT[:, b + 1:b + 1152],
                                                              mu[:, cc:cc + 1], mix_sc[:, 0:1151],
                                                              ALU.mult, ALU.add)
                            r = OPI("vector", "DV", _b)
                        else:
                            def _a(E, b=b, mu=mu, om=om, cc=cc, shoff=shoff):
                                return E.tensor_scalar(mix_sc[:, 0:1152],
                                                       ynT[:, b + shoff - 1:b + shoff - 1 + 1152],
                                                       om[:, cc:cc + 1], None, ALU.mult)
                            OPIW("vector", "DV", _a)
                            def _b(E, b=b, o=o, mu=mu, cc=cc, shoff=shoff):
                                return E.scalar_tensor_tensor(xmix[:, o:o + 1152],
                                                              ynT[:, b + shoff:b + shoff + 1152],
                                                              mu[:, cc:cc + 1], mix_sc[:, 0:1152],
                                                              ALU.mult, ALU.add)
                            r = OPI("vector", "DV", _b)
                    return r

                dv_mx = mix_one(mu_k, om_k)
                W("tensor", "DV", dv_mx)
                W("tensor", "AC", ac_sig_prev)
                if sp == 0:
                    W("tensor", "SG", sg_fkr)
                ac_hist = [0, 0]
                dv_k = 0
                for dd in range(8):
                    for ncb in range(3):
                        n0 = 512 * ncb
                        nw = min(512, 1152 - n0)
                        slot = (dd * 3 + ncb) % 2
                        pslc = psB[:, 0:nw] if slot == 0 else psA[:, 0:nw]
                        W("tensor", "AC", ac_hist[slot])
                        def kmm(E, dd=dd, n0=n0, nw=nw, pslc=pslc):
                            for cc in range(8):
                                r = E.matmul(pslc, Fk_sb[:, cc, 128 * dd:128 * dd + 128],
                                             xmix[:, 1152 * cc + n0:1152 * cc + n0 + nw],
                                             start=(cc == 0), stop=(cc == 7))
                            return r
                        pe_k = OPI("tensor", "PE", kmm)
                        W("scalar", "PE", pe_k)
                        W("scalar", "DV", dv_k)
                        ac_k = OPI("scalar", "AC", lambda E, pslc=pslc, nw=nw: E.activation(
                            krl[:, 0:nw], pslc, AF.Relu, bias=0.0, scale=1.0))
                        ac_hist[slot] = ac_k
                        W("vector", "AC", ac_k)
                        dv_k = OPI("vector", "DV", lambda E, dd=dd, n0=n0, nw=nw: E.tensor_mul(
                            kaT[:, 1152 * dd + n0:1152 * dd + n0 + nw],
                            krl[:, 0:nw], krl[:, 0:nw]))
                pe_kdone = sch.cnt["PE"]
                W("vector", "PE", pe_kdone)
                dv_mr = mix_one(mu_r, om_r)
                W("tensor", "DV", dv_k)
                W("tensor", "DV", dv_mr)
                if sp == 0:
                    W("tensor", ready_sem, max(rv))
                    W("tensor", "SG", sg_fv)
                ac_sig = 0
                ac_sig_h = [0, 0]
                dv_rv = 0
                for tl in range(9):
                    t = 9 * sp + tl
                    rp = psD[:, 0:1024] if t % 2 == 0 else psA[:, 0:1024]
                    W("tensor", "AC", ac_sig_h[t % 2])
                    def rmm(E, tl=tl, rp=rp):
                        for cc in range(8):
                            for k in range(2):
                                r = E.matmul(rp[:, 512 * k:512 * (k + 1)],
                                             xmix[:, 1152 * cc + 128 * tl:1152 * cc + 128 * tl + 128],
                                             Fr_sb[:, cc, 512 * k:512 * (k + 1)],
                                             start=(cc == 0), stop=(cc == 7))
                        return r
                    pe_r = OPI("tensor", "PE", rmm)
                    W("scalar", "PE", pe_r)
                    W("scalar", "DV", dv_rv)
                    def sig(E, rp=rp):
                        E.activation(rsig[:, 0:512], rp[:, 0:512], AF.Sigmoid, bias=0.0, scale=1.0)
                        return E.activation(rsig[:, 512:1024], rp[:, 512:1024], AF.Sigmoid, bias=0.0, scale=1.0)
                    ac_sig = OPI("scalar", "AC", sig)
                    ac_sig_h[t % 2] = ac_sig
                    ac_sig_prev = ac_sig
                    W("tensor", "DV", dv_rv)
                    def vmm2(E, tl=tl):
                        for cc in range(8):
                            for k in range(2):
                                r = E.matmul(psA[:, 1024 + 512 * k:1024 + 512 * (k + 1)],
                                             kaT[:, 1152 * cc + 128 * tl:1152 * cc + 128 * tl + 128],
                                             Fv_sb[:, cc, 512 * k:512 * (k + 1)],
                                             start=(cc == 0), stop=(cc == 7))
                        return r
                    pe_v = OPI("tensor", "PE", vmm2)
                    W("vector", "PE", pe_v)
                    W("vector", "AC", ac_sig)
                    W("vector", "PL", last_pl)
                    def rv(E):
                        E.tensor_mul(tv_sb[:, 0:512], psA[:, 1024:1536], rsig[:, 0:512])
                        return E.tensor_mul(tv_sb[:, 512:1024], psA[:, 1536:2048], rsig[:, 512:1024])
                    dv_rv = OPI("vector", "DV", rv)
                    W("gpsimd", "DV", dv_rv)
                    W("gpsimd", "PL", last_pl)
                    last_pl = OPI("gpsimd", "PL", lambda E, t=t: E.tensor_add(
                        stream[:, 1024 * t:1024 * (t + 1)],
                        stream[:, 1024 * t:1024 * (t + 1)], tv_sb[:, :]))
                    pl_t[t] = last_pl
                pe_sp = sch.cnt["PE"]
            return pl_t

        es5.close()
        with contextlib.ExitStack() as es6:
            pl_x2_t = ffn_phase("DV", x1_dv_t, es6)
            pl_x2 = max(pl_x2_t)
        if "x2" in dump:
            dump_stream("x2", "PL", pl_x2)

        es7 = contextlib.ExitStack()
        def sb7(name, shape, dt):
            return es7.enter_context(nc.sbuf_tensor(name, shape, dt))
        y3n = sb7("y3n", [128, NT * 1024], BF16)
        y3nT = sb7("y3nT", [128, 8 * 1408], BF16)
        AT_sb = sb7("AT_sb", [128, 8 * 1408], BF16)
        bmask = sb7("bmask_sb", [128, 3, 384], BF16)
        sc_sb = sb7("sc_sb", [128, 384], F32)
        negb = sb7("negb", [128, 1], F32)
        rcpb = sb7("rcpb", [128, 1], F32)
        EbT = sb7("EbT", [128, 384], BF16)
        ra_sb = sb7("ra_sb", [128, 1024], F32)
        y3f = sb7("y3f", [128, 1024], F32)
        Wb_sb = sb7("Wb_sb", [128, 8, 1024], BF16)

        sg_wb = DMA("gpsimd", "SG", Wb_sb[:, :, :], WbI.rearrange("(c p) d -> p c d", p=128))
        W("sync", "PL", pl_x2)
        bmv = DMA("sync", "SQ", bmask[:, :, :], bmask_in.transpose([1, 0, 2]))

        for t in range(NT):
            W("vector", "PL", pl_x2_t[t])
            dv_st3 = ln_stats(stream[:, 1024 * t:1024 * (t + 1)], t, negm3, var_s)
        W("scalar", "DV", dv_st3)
        ac_r3 = rstd_batch(var_s, rstd3, NT)
        W("vector", "AC", ac_r3)
        ywv_t7 = [0] * NT
        for t in range(NT):
            dv_a = OPI("vector", "DV", lambda E, t=t: E.tensor_scalar(
                y3n[:, 1024 * t:1024 * (t + 1)], stream[:, 1024 * t:1024 * (t + 1)],
                negm3[:, t:t + 1], rstd3[:, t:t + 1], ALU.add, ALU.mult))
            W("sync", "DV", dv_a)
            ywv_t7[t] = DMA("sync", "SQ", yb_d[128 * t:128 * (t + 1), :], y3n[:, 1024 * t:1024 * (t + 1)])

        pe_sp7 = 0
        dv_ra = 0
        ac_ebt = 0
        pe_t7 = 0
        dv_sc = 0
        pl_x3 = 0
        pl_x3_t = [0] * NT
        for sp in range(2):
            lo = 0 if sp == 0 else 1024
            span = 1408 if sp == 0 else 1280
            W("sync", "SQ", ywv_t7[10 if sp == 0 else 17])
            W("sync", "PE", pe_sp7)
            for cc in range(8):
                tv_ = DMA("sync", "SQ", y3nT[:, 1408 * cc:1408 * cc + span],
                          yb_d[lo:lo + span, 128 * cc:128 * (cc + 1)], transpose=True)
            W("tensor", "SQ", tv_)
            if sp == 0:
                W("tensor", "SG", sg_wb)
            ac_hist = [0, 0]
            ac_a = 0
            nchunks = [(0, 512), (512, 512), (1024, span - 1024)]
            for ee in range(8):
                for ci, (n0, nw) in enumerate(nchunks):
                    slot = (ee * 3 + ci) % 2
                    pslc = psB[:, 0:nw] if slot == 0 else psA[:, 0:nw]
                    W("tensor", "AC", ac_hist[slot])
                    def amm(E, ee=ee, n0=n0, nw=nw, pslc=pslc):
                        for cc in range(8):
                            r = E.matmul(pslc, Wb_sb[:, cc, 128 * ee:128 * ee + 128],
                                         y3nT[:, 1408 * cc + n0:1408 * cc + n0 + nw],
                                         start=(cc == 0), stop=(cc == 7))
                        return r
                    pe_a = OPI("tensor", "PE", amm)
                    W("scalar", "PE", pe_a)
                    ac_a = OPI("scalar", "AC", lambda E, ee=ee, n0=n0, nw=nw, pslc=pslc: E.copy(
                        AT_sb[:, 1408 * ee + n0:1408 * ee + n0 + nw], pslc))
                    ac_hist[slot] = ac_a
            if sp == 0:
                W("tensor", "SQ", bmv)
            qts = range(0, 9) if sp == 0 else range(9, 18)
            for t in qts:
                w0 = 0 if t == 0 else (1920 if t == 17 else 128 * (t - 1))
                mv = 0 if t == 0 else (2 if t == 17 else 1)
                W("tensor", "DV", dv_sc)
                W("tensor", "AC", ac_a)
                def smm7(E, t=t, w0=w0, lo=lo):
                    for cc in range(8):
                        r = E.matmul(psB[:, 0:384],
                                     AT_sb[:, 1408 * cc + 128 * t - lo:1408 * cc + 128 * t - lo + 128],
                                     y3nT[:, 1408 * cc + w0 - lo:1408 * cc + w0 - lo + 384],
                                     start=(cc == 0), stop=(cc == 7))
                    return r
                pe_s7 = OPI("tensor", "PE", smm7)
                W("vector", "PE", pe_s7)
                OPIW("vector", "DV", lambda E, mv=mv: E.tensor_add(
                    sc_sb[:, :], psB[:, 0:384], bmask[:, mv, :]))
                dv_sc = OPI("vector", "DV", lambda E: E.reduce_max(
                    negb[:, :], sc_sb[:, :], axis=AXX, negate=True))
                W("scalar", "DV", dv_sc)
                W("scalar", "PE", pe_t7)
                ac_e7 = OPI("scalar", "AC", lambda E: E.activation(
                    etri[:, 0:384], sc_sb[:, :], AF.Exp, bias=negb[:, :], scale=1.0,
                    accum_out=rcpb[:, :]))
                W("vector", "AC", ac_e7)
                dv_rc = OPI("vector", "DV", lambda E: E.reciprocal(rcpb[:, :], rcpb[:, :]))
                W("tensor", "AC", ac_e7)
                W("tensor", "AC", ac_ebt)
                def tmm(E):
                    for kbl in range(3):
                        r = E.transpose(psE[:, 128 * kbl:128 * (kbl + 1)],
                                        etri[:, 128 * kbl:128 * (kbl + 1)], ident[:, :])
                    return r
                pe_t7 = OPI("tensor", "PE", tmm)
                W("scalar", "PE", pe_t7)
                ac_ebt = OPI("scalar", "AC", lambda E: E.copy(EbT[:, 0:384], psE[:, 0:384]))
                W("tensor", "AC", ac_ebt)
                W("tensor", "DV", dv_ra)
                def pv7(E, w0=w0):
                    kt0 = w0 // 128
                    for kbl in range(3):
                        for k in range(2):
                            r = E.matmul(psD[:, 512 * k:512 * (k + 1)],
                                         EbT[:, 128 * kbl:128 * kbl + 128],
                                         y3n[:, 1024 * (kt0 + kbl) + 512 * k:1024 * (kt0 + kbl) + 512 * (k + 1)],
                                         start=(kbl == 0), stop=(kbl == 2))
                    return r
                pe_pv7 = OPI("tensor", "PE", pv7)
                W("vector", "PE", pe_pv7)
                W("vector", "DV", dv_rc)
                W("vector", "PL", pl_x3)
                def rafin(E, t=t):
                    E.tensor_scalar(ra_sb[:, 0:512], psD[:, 0:512], rcpb[:, :], 0.0, ALU.mult, ALU.max)
                    E.tensor_scalar(ra_sb[:, 512:1024], psD[:, 512:1024], rcpb[:, :], 0.0, ALU.mult, ALU.max)
                    return E.tensor_scalar(y3f[:, :], stream[:, 1024 * t:1024 * (t + 1)],
                                           negm3[:, t:t + 1], rstd3[:, t:t + 1], ALU.add, ALU.mult)
                dv_ra = OPI("vector", "DV", rafin)
                W("gpsimd", "DV", dv_ra)
                W("gpsimd", "PL", pl_x3)
                def x3add(E, t=t):
                    E.tensor_add(stream[:, 1024 * t:1024 * (t + 1)],
                                 stream[:, 1024 * t:1024 * (t + 1)], y3f[:, :])
                    return E.tensor_add(stream[:, 1024 * t:1024 * (t + 1)],
                                        stream[:, 1024 * t:1024 * (t + 1)], ra_sb[:, :])
                pl_x3 = OPI("gpsimd", "PL", x3add)
                pl_x3_t[t] = pl_x3
            pe_sp7 = sch.cnt["PE"]

        if "x3" in dump:
            dump_stream("x3", "PL", pl_x3)

        es7.close()
        with contextlib.ExitStack() as es8:
            pl_x4_t = ffn_phase("PL", pl_x3_t, es8)

        ov = 0
        for t in range(NT):
            W("sync", "PL", pl_x4_t[t])
            ov = DMA("sync", "SQ", out[128 * t:128 * (t + 1), :], stream[:, 1024 * t:1024 * (t + 1)])
        W("sync", "SQ", ov)

        @blockctx.tensor
        def _(E):
            for fn in sch.ops["tensor"]:
                fn(E)

        @blockctx.scalar
        def _(E):
            for fn in sch.ops["scalar"]:
                fn(E)

        @blockctx.vector
        def _(E):
            for fn in sch.ops["vector"]:
                fn(E)

        @blockctx.gpsimd
        def _(E):
            for fn in sch.ops["gpsimd"]:
                fn(E)

        @blockctx.sync
        def _(E):
            for fn in sch.ops["sync"]:
                fn(E)

    return nc


def _rope_tables():
    inv = (1.0 / (np.float32(10000.0) ** (np.arange(0, 64, 2, dtype=np.float32) / np.float32(64.0)))).astype(np.float32)
    ang = np.arange(S, dtype=np.float32)[:, None] * inv[None, :]
    cos = np.cos(ang).astype(np.float32)
    sin = np.sin(ang).astype(np.float32)
    cosT = np.empty((128, S), np.float32)
    sgnT = np.empty((128, S), np.float32)
    for r in range(128):
        j = r % 32
        cosT[r] = cos[:, j]
        sgnT[r] = (-sin[:, j]) if (r % 64) < 32 else sin[:, j]
    bf = ml_dtypes.bfloat16
    return np.ascontiguousarray(cosT.astype(bf)), np.ascontiguousarray(sgnT.astype(bf))


def _perm_rows(h0):
    n, r = [], []
    for h in range(h0, h0 + 4):
        ev = [64 * h + 2 * j for j in range(32)]
        od = [64 * h + 2 * j + 1 for j in range(32)]
        n += ev + od
        r += od + ev
    return np.array(n), np.array(r)


def _bmasks():
    m = np.zeros((3, 128, 384), np.float32)
    for v, d0 in ((0, 0), (1, 128), (2, 256)):
        i = np.arange(128)[:, None]
        j = np.arange(384)[None, :]
        delta = d0 + i - j
        m[v] = np.where(np.abs(delta) <= 64, 0.0, -1e9)
    return np.ascontiguousarray(m.astype(ml_dtypes.bfloat16))


def kernel(L0_attn_a_x, L0_attn_a_wq, L0_attn_a_wk, L0_attn_a_wv, L0_attn_a_wo,
           L0_attn_b_W, L0_ffn_W_r, L0_ffn_W_k, L0_ffn_W_v, L0_ffn_mu_r, L0_ffn_mu_k):
    x = np.ascontiguousarray(np.asarray(L0_attn_a_x, dtype=np.float32))
    wq = np.asarray(L0_attn_a_wq, np.float32)
    wk = np.asarray(L0_attn_a_wk, np.float32)
    wv = np.asarray(L0_attn_a_wv, np.float32)
    wo = np.asarray(L0_attn_a_wo, np.float32)
    Wb = np.asarray(L0_attn_b_W, np.float32)
    Fr = np.asarray(L0_ffn_W_r, np.float32)
    Fk = np.asarray(L0_ffn_W_k, np.float32)
    Fv = np.asarray(L0_ffn_W_v, np.float32)
    mu_r = np.asarray(L0_ffn_mu_r, np.float32)
    mu_k = np.asarray(L0_ffn_mu_k, np.float32)

    bf = ml_dtypes.bfloat16
    cosT, sgnT = _rope_tables()
    tri = np.ascontiguousarray(
        (np.arange(128)[:, None] <= np.arange(128)[None, :]).astype(np.float32).astype(bf))
    bm = _bmasks()
    ident = np.ascontiguousarray(np.eye(128, dtype=np.float32).astype(bf))
    woT = np.ascontiguousarray(wo.T)
    FrT = np.ascontiguousarray(Fr.T)
    FkT = np.ascontiguousarray(Fk.T)
    FvT = np.ascontiguousarray(Fv.T)
    mu_r2 = np.ascontiguousarray(mu_r.reshape(8, 128).T)
    mu_k2 = np.ascontiguousarray(mu_k.reshape(8, 128).T)

    in_maps = []
    for c in range(8):
        b, g = c // 4, c % 4
        off = OFFS[g]
        rn, rr = _perm_rows(4 * g)
        in_maps.append({
            "x_full": x[b],
            "x_slice": np.ascontiguousarray(x[b, off:off + WIN]),
            "wqT_n": np.ascontiguousarray(wq[rn].T),
            "wqT_r": np.ascontiguousarray(wq[rr].T),
            "wkT_n": np.ascontiguousarray(wk[rn].T),
            "wkT_r": np.ascontiguousarray(wk[rr].T),
            "wvT": np.ascontiguousarray(wv[4 * g * 64:(4 * g + 4) * 64].T),
            "woT": woT,
            "FrT": FrT, "FkT": FkT, "FvT": FvT,
            "Wb": Wb,
            "mu_r": mu_r2, "mu_k": mu_k2,
            "rope_cos": cosT, "rope_sgn": sgnT,
            "tri": tri, "bmask": bm, "ident": ident,
            "goff": np.array([[off]], dtype=np.int32),
        })

    nc = build_program()
    res = run_bass_kernel_spmd(nc, in_maps, core_ids=list(range(8)))
    global _last_res
    _last_res = res

    outp = np.empty((2, S, DM), np.float32)
    for c in range(8):
        b, g = c // 4, c % 4
        o = np.asarray(res.results[c]["out"], np.float32)
        outp[b, 2048 * g:2048 * (g + 1)] = o[R0[g]:R0[g] + 2048]
    return outp


# version 18
# speedup vs baseline: 1.0446x; 1.0095x over previous
import contextlib
import dataclasses
import numpy as np
import ml_dtypes
import concourse.bass as bass
import concourse.mybir as mybir
from concourse.bass_utils import run_bass_kernel_spmd

F32 = mybir.dt.float32
BF16 = mybir.dt.bfloat16
I32 = mybir.dt.int32
AF = mybir.ActivationFunctionType
ALU = mybir.AluOpType
AXX = mybir.AxisListType.X

S = 8192
DM = 1024
WIN = 2304
NT = 18
NH = 4
OFFS = [0, 1920, 3968, 5888]
R0 = [0, 128, 128, 256]

ENGINES = ("tensor", "scalar", "vector", "gpsimd", "sync")


class Sched:
    def __init__(self):
        self.ops = {e: [] for e in ENGINES}
        self.cnt = {}

    def emit(self, eng, fn):
        self.ops[eng].append(fn)

    def inc(self, sem, k=1):
        self.cnt[sem] = self.cnt.get(sem, 0) + k
        return self.cnt[sem]


def build_program(dump=()):
    nc = bass.Bass("TRN2", target_bir_lowering=False)

    def din(name, shape, dt=F32):
        return nc.dram_tensor(name, shape, dt, kind="ExternalInput").ap()

    x_full = din("x_full", [S, DM])
    x_slice = din("x_slice", [WIN, DM])
    wqT_n = din("wqT_n", [DM, 256]); wqT_r = din("wqT_r", [DM, 256])
    wkT_n = din("wkT_n", [DM, 256]); wkT_r = din("wkT_r", [DM, 256])
    wvT = din("wvT", [DM, 256])
    woT = din("woT", [DM, DM])
    FrT = din("FrT", [DM, DM]); FkT = din("FkT", [DM, DM]); FvT = din("FvT", [DM, DM])
    WbI = din("Wb", [DM, DM])
    mu_r_in = din("mu_r", [128, 8]); mu_k_in = din("mu_k", [128, 8])
    rope_cos = din("rope_cos", [128, S], BF16)
    rope_sgn = din("rope_sgn", [128, S], BF16)
    tri_in = din("tri", [128, 128], BF16)
    bmask_in = din("bmask", [3, 128, 384], BF16)
    ident_in = din("ident", [128, 128], BF16)
    goff = din("goff", [1, 1], I32)

    out = nc.dram_tensor("out", [WIN, DM], F32, kind="ExternalOutput").ap()

    dumps = {}
    def dout(name, shape, dt):
        dumps[name] = nc.dram_tensor("d_" + name, shape, dt, kind="ExternalOutput").ap()
        return dumps[name]

    y1n_d = nc.dram_tensor("y1n_d", [S, DM], BF16).ap()
    yb_d = nc.dram_tensor("yb_d", [WIN, DM], BF16).ap()
    ag_in = nc.dram_tensor("ag_in", [256, S], BF16).ap()
    ag1_out = nc.dram_tensor("ag1_out", [512, S], BF16).ap()
    ag2_out = nc.dram_tensor("ag2_out", [512, S], BF16).ap()

    sch = Sched()
    es = contextlib.ExitStack()
    sems = {}

    def W(eng, s, v):
        if v <= 0:
            return
        sch.emit(eng, lambda E, h=sems[s], v=v: E.wait_ge(h, v))

    def DMA(eng, s, o, i, **kw):
        v = sch.inc(s, 16)
        sch.emit(eng, lambda E, o=o, i=i, h=sems[s], kw=kw: E.dma_start(out=o, in_=i, **kw).then_inc(h, 16))
        return v

    def OP(eng, fn):
        sch.emit(eng, fn)

    def OPI(eng, s, fn):
        v = sch.inc(s, 1)
        sch.emit(eng, lambda E, fn=fn, h=sems[s]: fn(E).then_inc(h, 1))
        return v

    def OPIW(eng, s, fn):
        v = OPI(eng, s, fn)
        W(eng, s, v)
        return v

    with es:
        for sname in ("SQ", "SG", "PE", "AC", "DV", "PL", "CCA", "CCB"):
            sems[sname] = es.enter_context(nc.semaphore(sname))

        def sb(name, shape, dt):
            return es.enter_context(nc.sbuf_tensor(name, shape, dt))

        tri = sb("tri_sb", [128, 128], BF16)
        ident = sb("ident_sb", [128, 128], BF16)
        ones1 = sb("ones1", [1, 128], F32)
        bias3 = sb("bias3", [128, 1], F32)
        eps_t = sb("eps_t", [128, 1], F32)
        mu_r = sb("mu_r_sb", [128, 8], F32); mu_k = sb("mu_k_sb", [128, 8], F32)
        om_r = sb("om_r", [128, 8], F32); om_k = sb("om_k", [128, 8], F32)
        off_sb = sb("off_sb", [1, 1], I32)
        negm3 = sb("negm3", [128, NT], F32)
        rstd3 = sb("rstd3", [128, NT], F32)
        negm_s = sb("negm_s", [128, NT], F32)
        var_s = sb("var_s", [128, NT], F32)
        rstd_s = sb("rstd_s", [128, NT], F32)
        bnst = sb("bnst", [128, 12], F32)
        bnsc = sb("bnsc", [128, 8], F32)
        etri = sb("etri", [128, 512], BF16)

        es_a = contextlib.ExitStack()
        def sba(name, shape, dt):
            return es_a.enter_context(nc.sbuf_tensor(name, shape, dt))
        wq_n = sba("wq_n", [128, 8, 256], BF16)
        wq_r = sba("wq_r", [128, 8, 256], BF16)
        wk_n = sba("wk_n", [128, 8, 256], BF16)
        wk_r = sba("wk_r", [128, 8, 256], BF16)
        wv_s = sba("wv_s", [128, 8, 256], BF16)
        QT = [sba(f"QT{g}", [128, S], BF16) for g in range(2)]
        KT = [sba(f"KT{g}", [128, S], BF16) for g in range(2)]
        Vaug = [sba(f"Vaug{h}", [128, 64 * 65], BF16) for h in range(NH)]

        psA = es.enter_context(nc.psum_tensor("psA", [128, 2048], F32))
        psB = es.enter_context(nc.psum_tensor("psB", [128, 512], F32))
        psD = es.enter_context(nc.psum_tensor("psD", [128, 1024], F32))
        psE = es.enter_context(nc.psum_tensor("psE", [128, 512], BF16))

        blockctx = es.enter_context(nc.Block())

        for dst, src in ((wq_n, wqT_n), (wq_r, wqT_r), (wk_n, wkT_n), (wk_r, wkT_r), (wv_s, wvT)):
            sg_w0 = DMA("gpsimd", "SG", dst[:, :, :], src.rearrange("(c p) d -> p c d", p=128))
        DMA("sync", "SQ", tri[:, :], tri_in)
        DMA("sync", "SQ", ident[:, :], ident_in)
        DMA("sync", "SQ", mu_r[:, :], mu_r_in)
        DMA("sync", "SQ", mu_k[:, :], mu_k_in)
        sq_c0 = DMA("sync", "SQ", off_sb[:, :], goff)

        def _memsets(E):
            E.memset(ones1[:, :], 1.0)
            E.memset(ones1b[:, :], 1.0)
            E.memset(bias3[:, :], -3.0)
            for h in range(NH):
                E.memset(Vaug[h][:, :], 1.0)
            return E.memset(eps_t[:, :], 1e-5)
        OPI("vector", "DV", _memsets)
        W("vector", "SQ", sq_c0)
        OP("vector", lambda E: E.tensor_scalar(om_r[:, :], mu_r[:, :], -1.0, 1.0, ALU.mult, ALU.add))
        dv_c0 = OPI("vector", "DV", lambda E: E.tensor_scalar(om_k[:, :], mu_k[:, :], -1.0, 1.0, ALU.mult, ALU.add))

        def ln_stats(xap, col, negm_t, var_t):
            def s1(E, xap=xap):
                E.bn_stats(bnst[:, 0:6], xap[:, 0:512])
                return E.bn_stats(bnst[:, 6:12], xap[:, 512:1024])
            OPIW("vector", "DV", s1)
            def s2(E, col=col, negm_t=negm_t, var_t=var_t):
                E.reduce_sum(negm_t[:, col:col + 1], bnst[:, 1:12:3], axis=AXX, negate=True)
                E.reduce_sum(var_t[:, col:col + 1], bnst[:, 2:12:3], axis=AXX)
                return E.tensor_mul(bnsc[:, 1:4], bnst[:, 4:12:3], bnst[:, 4:12:3])
            OPIW("vector", "DV", s2)
            def s3(E, col=col, negm_t=negm_t, var_t=var_t):
                E.reduce_sum(bnsc[:, 4:5], bnsc[:, 1:4], axis=AXX)
                E.tensor_mul(bnsc[:, 5:6], bnst[:, 1:2], bnst[:, 1:2])
                E.tensor_mul(bnsc[:, 6:7], negm_t[:, col:col + 1], negm_t[:, col:col + 1])
                return E.tensor_scalar_mul(var_t[:, col:col + 1], var_t[:, col:col + 1], 1.0 / 1024.0)
            OPIW("vector", "DV", s3)
            def s4(E, col=col, negm_t=negm_t):
                E.tensor_add(bnsc[:, 7:8], bnsc[:, 4:5], bnsc[:, 5:6])
                E.tensor_scalar_mul(negm_t[:, col:col + 1], negm_t[:, col:col + 1], 0.25)
                return E.tensor_scalar_mul(bnsc[:, 6:7], bnsc[:, 6:7], 1.0 / 16.0)
            OPIW("vector", "DV", s4)
            def s5(E, col=col, var_t=var_t):
                return E.scalar_tensor_tensor(bnsc[:, 7:8], bnsc[:, 7:8], 0.25,
                                              var_t[:, col:col + 1], ALU.mult, ALU.add)
            OPIW("vector", "DV", s5)
            def s6(E, col=col, var_t=var_t):
                return E.tensor_sub(var_t[:, col:col + 1], bnsc[:, 7:8], bnsc[:, 6:7])
            return OPI("vector", "DV", s6)

        def rstd_batch(var_t, rstd_t, n):
            def _f1(E, var_t=var_t, rstd_t=rstd_t, n=n):
                return E.activation(rstd_t[:, 0:n], var_t[:, 0:n], AF.Ln, bias=eps_t[:, :], scale=1.0)
            OPIW("scalar", "AC", _f1)
            def _f2(E, rstd_t=rstd_t, n=n):
                return E.activation(rstd_t[:, 0:n], rstd_t[:, 0:n], AF.Exp, bias=0.0, scale=-0.5)
            return OPI("scalar", "AC", _f2)

        rs_sb = [sba(f"rs{i}", [1, 512], F32) for i in range(4)]
        rsb_sb = [sba(f"rsb{i}", [1, 512], BF16) for i in range(4)]
        ones1b = sba("ones1b", [1, 64], BF16)
        ou_sb = [sba(f"ou{i}", [64, 512], F32) for i in range(4)]
        ostg = [sba(f"ostg{i}", [64, 512], BF16) for i in range(4)]
        Eb_s = [sba(f"Ebuf{i}", [128, 1024], BF16) for i in range(4)]

        pv_hist = [0, 0]
        pv_buf = [0, 0, 0, 0]
        exp_hist = [0, 0]
        msk_dv = [0, 0]
        psd_free = [dv_c0, 0]
        norm_dv = [dv_c0] * 4
        stg_dma = [0] * 4
        stg_last = [0, 0]
        kv_pl = 0
        kv_ac = 0
        deferred = []

        def emit_attn_batch(qbs):
            W("tensor", "PL", kv_pl)
            W("tensor", "AC", kv_ac)
            for g2 in range(2):
                for qb in qbs:
                    qs = 512 * qb
                    nkb = 4 * (qb + 1)
                    pend = None

                    def emit_pv(args, last, qb=qb):
                        (kb, reg, diag, col0, first, eb) = args
                        W("tensor", "AC", exp_hist[reg])
                        if diag:
                            W("tensor", "DV", msk_dv[reg])
                        if first:
                            W("tensor", "DV", psd_free[0])
                            W("tensor", "AC", psd_free[1])
                        def pvmm(E, g2=g2, kb=kb, diag=diag, col0=col0, eb=eb,
                                 first=first, last=last):
                            r = None
                            for hh in range(2):
                                hcol = 512 * hh
                                V = Vaug[2 * g2 + hh]
                                if diag:
                                    r = E.matmul(psD[0:65, hcol + col0:hcol + col0 + 128],
                                                 V[:, 65 * kb:65 * kb + 65],
                                                 etri[:, 256 * (kb % 2) + 128 * hh:256 * (kb % 2) + 128 * hh + 128],
                                                 start=first, stop=last)
                                    if col0 + 128 < 512:
                                        r = E.matmul(psD[0:65, hcol + col0 + 128:hcol + 512],
                                                     V[:, 65 * kb:65 * kb + 65],
                                                     eb[:, hcol + col0 + 128:hcol + 512],
                                                     start=first, stop=False)
                                else:
                                    r = E.matmul(psD[0:65, hcol:hcol + 512],
                                                 V[:, 65 * kb:65 * kb + 65],
                                                 eb[:, hcol:hcol + 512],
                                                 start=first, stop=False)
                            return r
                        v = OPI("tensor", "PE", pvmm)
                        pv_hist[reg] = v
                        pv_buf[kb % 4] = v
                        return v

                    for kb in range(nkb):
                        reg = kb % 2
                        r0c = 1024 * reg
                        diag = kb >= 4 * qb
                        col0 = 128 * (kb - 4 * qb) if diag else 0
                        W("tensor", "AC", exp_hist[reg])
                        def smm(E, g2=g2, kb=kb, qs=qs, col0=col0, r0c=r0c):
                            E.matmul(psA[:, r0c + col0:r0c + 512],
                                     KT[g2][0:64, 128 * kb:128 * kb + 128],
                                     QT[g2][0:64, qs + col0:qs + 512],
                                     start=True, stop=True)
                            return E.matmul(psA[:, r0c + 512 + col0:r0c + 1024],
                                            KT[g2][64:128, 128 * kb:128 * kb + 128],
                                            QT[g2][64:128, qs + col0:qs + 512],
                                            start=True, stop=True)
                        pe_s = OPI("tensor", "PE", smm)
                        if kb == 1 and deferred:
                            for fn in deferred:
                                fn()
                            deferred.clear()
                        W("scalar", "PE", pe_s)
                        W("scalar", "PE", pv_buf[kb % 4])
                        eb = Eb_s[kb % 4]
                        ac_e = OPI("scalar", "AC", lambda E, eb=eb, r0c=r0c: E.activation(
                            eb[:, 0:1024], psA[:, r0c:r0c + 1024], AF.Exp,
                            bias=bias3[:, :], scale=0.125))
                        exp_hist[reg] = ac_e
                        if diag:
                            W("vector", "AC", ac_e)
                            W("vector", "PE", pv_hist[reg])
                            def msk(E, eb=eb, col0=col0, kb=kb):
                                E.tensor_mul(etri[:, 256 * (kb % 2):256 * (kb % 2) + 128],
                                             eb[:, col0:col0 + 128], tri[:, :])
                                return E.tensor_mul(etri[:, 256 * (kb % 2) + 128:256 * (kb % 2) + 256],
                                                    eb[:, 512 + col0:512 + col0 + 128], tri[:, :])
                            msk_dv[reg] = OPI("vector", "DV", msk)
                        if pend is not None:
                            emit_pv(pend, last=False)
                        pend = (kb, reg, diag, col0, kb == 0, eb)
                    pv_last = emit_pv(pend, last=True)

                    dv_rs = [0, 0]
                    ac_os = [0, 0]
                    for hh in range(2):
                        slot = 2 * (qb % 2) + hh
                        hcol = 512 * hh
                        W("vector", "PE", pv_last)
                        W("vector", "DV", norm_dv[slot])
                        OPIW("vector", "DV", lambda E, slot=slot, hcol=hcol: E.tensor_copy(
                            rs_sb[slot][:, :], psD[64:65, hcol:hcol + 512]))
                        OPIW("vector", "DV", lambda E, slot=slot: E.reciprocal(
                            rs_sb[slot][:, :], rs_sb[slot][:, :]))
                        dv_rs[hh] = OPI("vector", "DV", lambda E, slot=slot: E.tensor_copy(
                            rsb_sb[slot][:, :], rs_sb[slot][:, :]))
                        W("scalar", "PE", pv_last)
                        ac_os[hh] = OPI("scalar", "AC", lambda E, slot=slot, hcol=hcol: E.copy(
                            ou_sb[slot][:, :], psD[0:64, hcol:hcol + 512]))
                    psd_free[0] = sch.cnt["DV"]
                    psd_free[1] = ac_os[1]

                    def mk_tail(qb=qb, g2=g2, qs=qs, dv_rs=tuple(dv_rs), ac_os=tuple(ac_os)):
                        def tailf():
                            for hh in range(2):
                                slot = 2 * (qb % 2) + hh
                                W("tensor", "DV", dv_rs[hh])
                                W("tensor", "DV", norm_dv[slot])
                                pe_bc = OPI("tensor", "PE", lambda E, slot=slot, hh=hh: E.matmul(
                                    psB[64 * hh:64 * hh + 64, :], ones1b[:, :],
                                    rsb_sb[slot][:, :], start=True, stop=True))
                                W("vector", "PE", pe_bc)
                                W("vector", "AC", ac_os[hh])
                                W("vector", "SQ", stg_dma[slot])
                                def norm(E, hh=hh, slot=slot):
                                    return E.tensor_mul(ostg[slot][:, :],
                                                        ou_sb[slot][:, :], psB[64 * hh:64 * hh + 64, :])
                                norm_dv[slot] = OPI("vector", "DV", norm)
                                W("sync", "DV", norm_dv[slot])
                                stg_dma[slot] = DMA("sync", "SQ",
                                                    ag_in[128 * g2 + 64 * hh:128 * g2 + 64 * hh + 64, qs:qs + 512],
                                                    ostg[slot][:, :])
                                stg_last[g2] = stg_dma[slot]
                        return tailf
                    deferred.append(mk_tail())

        with contextlib.ExitStack() as es12:
            xch = [es12.enter_context(nc.sbuf_tensor(f"xch{i}", [128, 1024], F32)) for i in range(2)]
            xap = [es12.enter_context(nc.sbuf_tensor(f"xap{i}", [128, 1024], F32)) for i in range(2)]
            ych = [es12.enter_context(nc.sbuf_tensor(f"ych{i}", [128, 1024], BF16)) for i in range(2)]
            y1nT = [es12.enter_context(nc.sbuf_tensor(f"y1nT{c}", [128, 1024], BF16)) for c in range(8)]
            cos_st = es12.enter_context(nc.sbuf_tensor("cos_st", [128, 1024], BF16))
            sgn_st = es12.enter_context(nc.sbuf_tensor("sgn_st", [128, 1024], BF16))
            rsc = [[es12.enter_context(nc.sbuf_tensor(f"rsc{p}_{i}", [128, 512], BF16))
                    for i in range(4)] for p in range(2)]
            rtmp1 = es12.enter_context(nc.sbuf_tensor("rtmp1", [128, 512], F32))
            rtmp2 = es12.enter_context(nc.sbuf_tensor("rtmp2", [128, 512], F32))
            bnq = es12.enter_context(nc.sbuf_tensor("bnq", [128, 16, 12], F32))
            msq = es12.enter_context(nc.sbuf_tensor("msq", [128, 16, 4], F32))
            s2q = es12.enter_context(nc.sbuf_tensor("s2q", [128, 16], F32))
            w1q = es12.enter_context(nc.sbuf_tensor("w1q", [128, 16], F32))
            uq = es12.enter_context(nc.sbuf_tensor("uq", [128, 16], F32))

            xch_free = [0, 0]
            xap_free = [0, 0]
            ych_free = [0, 0]
            last_y1w = 0
            rope_done = 0
            evac_hist = [0, 0]
            rope_pl = [0, 0]
            v_evac_h = [0, 0]
            pe_qkv_done = 0

            def emit_ln_quarter(qs_):
                nonlocal last_y1w
                st_dv = 0
                for ct in range(16):
                    gc = 16 * qs_ + ct
                    bi = gc % 2
                    W("sync", "DV", xch_free[bi])
                    xv = DMA("sync", "SQ", xch[bi][:, :], x_full[128 * gc:128 * (gc + 1), :])
                    W("vector", "SQ", xv)
                    def st(E, bi=bi, ct=ct):
                        E.bn_stats(bnq[:, ct, 0:6], xch[bi][:, 0:512])
                        return E.bn_stats(bnq[:, ct, 6:12], xch[bi][:, 512:1024])
                    st_dv = OPI("vector", "DV", st)
                    xch_free[bi] = st_dv
                W("vector", "DV", st_dv)
                W("vector", "PL", max(xap_free))
                def cb1(E):
                    for t in range(16):
                        E.reduce_sum(negm_s[:, t:t + 1], bnq[:, t, 1:12:3], axis=AXX, negate=True)
                        E.reduce_sum(var_s[:, t:t + 1], bnq[:, t, 2:12:3], axis=AXX)
                        r = E.tensor_mul(msq[:, t, :], bnq[:, t, 1:12:3], bnq[:, t, 1:12:3])
                    return r
                OPIW("vector", "DV", cb1)
                def cb2(E):
                    for t in range(16):
                        r = E.reduce_sum(s2q[:, t:t + 1], msq[:, t, :], axis=AXX)
                    E.tensor_mul(uq[:, :], negm_s[:, 0:16], negm_s[:, 0:16])
                    return E.tensor_scalar_mul(w1q[:, :], var_s[:, 0:16], 1.0 / 1024.0)
                OPIW("vector", "DV", cb2)
                def cb3(E):
                    E.tensor_scalar_mul(negm_s[:, 0:16], negm_s[:, 0:16], 0.25)
                    return E.scalar_tensor_tensor(s2q[:, :], s2q[:, :], 0.25, w1q[:, :], ALU.mult, ALU.add)
                OPIW("vector", "DV", cb3)
                def cb4(E):
                    return E.scalar_tensor_tensor(var_s[:, 0:16], uq[:, :], -1.0 / 16.0,
                                                  s2q[:, :], ALU.mult, ALU.add)
                cb_dv = OPI("vector", "DV", cb4)
                W("scalar", "DV", cb_dv)
                ac_r = rstd_batch(var_s, rstd_s, 16)
                W("gpsimd", "DV", cb_dv)
                W("gpsimd", "AC", ac_r)
                for ct in range(16):
                    gc = 16 * qs_ + ct
                    bi = gc % 2
                    W("sync", "PL", xap_free[bi])
                    xv = DMA("sync", "SQ", xap[bi][:, :], x_full[128 * gc:128 * (gc + 1), :])
                    W("gpsimd", "SQ", xv)
                    W("gpsimd", "SQ", ych_free[bi])
                    pl_ap = OPI("gpsimd", "PL", lambda E, bi=bi, ct=ct: E.tensor_scalar(
                        ych[bi][:, :], xap[bi][:, :],
                        negm_s[:, ct:ct + 1], rstd_s[:, ct:ct + 1], ALU.add, ALU.mult))
                    xap_free[bi] = pl_ap
                    W("sync", "PL", pl_ap)
                    yv = DMA("sync", "SQ", y1n_d[128 * gc:128 * (gc + 1), :], ych[bi][:, :])
                    ych_free[bi] = yv
                    last_y1w = yv

            emit_ln_quarter(0)
            for qt in range(4):
                for half in range(2):
                    eh = 2 * qt + half
                    W("sync", "SQ", last_y1w)
                    W("sync", "PE", pe_qkv_done)
                    for cc in range(8):
                        tv = DMA("sync", "SQ", y1nT[cc][:, :],
                                 y1n_d[1024 * eh:1024 * (eh + 1), 128 * cc:128 * (cc + 1)], transpose=True)
                    W("sync", "PL", rope_done)
                    DMA("sync", "SQ", cos_st[:, :], rope_cos[:, 1024 * eh:1024 * (eh + 1)])
                    tab_done = DMA("sync", "SQ", sgn_st[:, :], rope_sgn[:, 1024 * eh:1024 * (eh + 1)])

                    W("tensor", "SQ", tab_done)
                    if qt == 0 and half == 0:
                        W("tensor", "SG", sg_w0)
                    W("tensor", "AC", max(exp_hist))
                    for g2 in range(2):
                        for nb in range(2):
                            par = nb % 2
                            W("tensor", "AC", evac_hist[0])
                            def qmm(E, g2=g2, nb=nb):
                                for sl, w in ((0, wq_n), (1, wq_r)):
                                    for cc in range(8):
                                        r = E.matmul(psA[:, 512 * sl:512 * (sl + 1)],
                                                     w[:, cc, 128 * g2:128 * g2 + 128],
                                                     y1nT[cc][:, 512 * nb:512 * nb + 512],
                                                     start=(cc == 0), stop=(cc == 7))
                                return r
                            pe_q = OPI("tensor", "PE", qmm)
                            W("tensor", "AC", evac_hist[1])
                            def kmm(E, g2=g2, nb=nb):
                                for sl, w in ((2, wk_n), (3, wk_r)):
                                    for cc in range(8):
                                        r = E.matmul(psA[:, 512 * sl:512 * (sl + 1)],
                                                     w[:, cc, 128 * g2:128 * g2 + 128],
                                                     y1nT[cc][:, 512 * nb:512 * nb + 512],
                                                     start=(cc == 0), stop=(cc == 7))
                                return r
                            pe_k = OPI("tensor", "PE", kmm)
                            W("scalar", "PE", pe_q)
                            W("scalar", "PL", rope_pl[par])
                            def evacq(E, par=par):
                                E.copy(rsc[par][0][:, :], psA[:, 0:512])
                                return E.copy(rsc[par][1][:, :], psA[:, 512:1024])
                            evac_hist[0] = OPI("scalar", "AC", evacq)
                            W("scalar", "PE", pe_k)
                            def evack(E, par=par):
                                E.copy(rsc[par][2][:, :], psA[:, 1024:1536])
                                return E.copy(rsc[par][3][:, :], psA[:, 1536:2048])
                            ac_ev = OPI("scalar", "AC", evack)
                            evac_hist[1] = ac_ev
                            col = 1024 * eh + 512 * nb
                            W("gpsimd", "AC", ac_ev)
                            def ropem(E, par=par, nb=nb):
                                E.tensor_mul(rtmp1[:, :], rsc[par][0][:, :], cos_st[:, 512 * nb:512 * nb + 512])
                                return E.tensor_mul(rtmp2[:, :], rsc[par][1][:, :], sgn_st[:, 512 * nb:512 * nb + 512])
                            v = OPI("gpsimd", "PL", ropem)
                            W("gpsimd", "PL", v)
                            OPI("gpsimd", "PL", lambda E, g2=g2, col=col: E.tensor_add(
                                QT[g2][:, col:col + 512], rtmp1[:, :], rtmp2[:, :]))
                            def ropem2(E, par=par, nb=nb):
                                E.tensor_mul(rtmp1[:, :], rsc[par][2][:, :], cos_st[:, 512 * nb:512 * nb + 512])
                                return E.tensor_mul(rtmp2[:, :], rsc[par][3][:, :], sgn_st[:, 512 * nb:512 * nb + 512])
                            v = OPI("gpsimd", "PL", ropem2)
                            W("gpsimd", "PL", v)
                            rope_pl[par] = OPI("gpsimd", "PL", lambda E, g2=g2, col=col: E.tensor_add(
                                KT[g2][:, col:col + 512], rtmp1[:, :], rtmp2[:, :]))
                    rope_done = max(rope_pl)

                    W("tensor", "AC", max(exp_hist))
                    for nb in range(2):
                        for tt in range(4):
                            kb = 8 * eh + 4 * nb + tt
                            vslc = psA[:, 0:256] if kb % 2 == 0 else psA[:, 512:768]
                            W("tensor", "AC", v_evac_h[kb % 2])
                            def vmm(E, nb=nb, tt=tt, vslc=vslc):
                                for cc in range(8):
                                    r = E.matmul(vslc,
                                                 y1nT[cc][:, 512 * nb + 128 * tt:512 * nb + 128 * tt + 128],
                                                 wv_s[:, cc, :], start=(cc == 0), stop=(cc == 7))
                                return r
                            pe_v = OPI("tensor", "PE", vmm)
                            W("scalar", "PE", pe_v)
                            def vev(E, kb=kb, vslc=vslc):
                                for h in range(NH):
                                    r = E.copy(Vaug[h][:, 65 * kb:65 * kb + 64],
                                               vslc[:, 64 * h:64 * h + 64])
                                return r
                            v_evac_h[kb % 2] = OPI("scalar", "AC", vev)
                    pe_qkv_done = sch.cnt["PE"]
                    if half == 0 and qt < 3:
                        emit_ln_quarter(qt + 1)
                    kv_pl = rope_done
                    kv_ac = max(v_evac_h)
                    emit_attn_batch(range(2 * eh, 2 * eh + 2))

            for fn in deferred:
                fn()
            deferred.clear()

            if "qkv" in dump:
                dqt = dout("QT", [2, 128, S], BF16)
                dkt = dout("KT", [2, 128, S], BF16)
                dva = dout("Vaug", [NH, 128, 64 * 65], BF16)
                W("sync", "PL", rope_done)
                W("sync", "AC", max(v_evac_h))
                for g in range(2):
                    DMA("sync", "SQ", dqt[g], QT[g][:, :])
                    DMA("sync", "SQ", dkt[g], KT[g][:, :])
                for h in range(NH):
                    DMA("sync", "SQ", dva[h], Vaug[h][:, :])
            if "y1" in dump:
                dy1 = dout("y1n", [S, DM], BF16)
                W("sync", "SQ", last_y1w)
                DMA("sync", "SQ", dy1, y1n_d)

        W("gpsimd", "SQ", stg_last[0])
        OPI("gpsimd", "CCA", lambda E: E.collective_compute(
            "AllGather", ALU.bypass,
            replica_groups=[[0, 1, 2, 3], [4, 5, 6, 7]],
            ins=[ag_in[0:128, :].opt()], outs=[ag1_out.opt()]))
        W("gpsimd", "SQ", stg_last[1])
        OPI("gpsimd", "CCB", lambda E: E.collective_compute(
            "AllGather", ALU.bypass,
            replica_groups=[[0, 1, 2, 3], [4, 5, 6, 7]],
            ins=[ag_in[128:256, :].opt()], outs=[ag2_out.opt()]))
        agv = max(stg_dma)

        if "attn" in dump:
            dat = dout("ag_in", [256, S], BF16)
            W("sync", "SQ", agv)
            DMA("sync", "SQ", dat, ag_in)

        es_a.close()
        stream = sb("stream", [128, NT * 1024], F32)
        es5 = contextlib.ExitStack()
        G_sb = [es5.enter_context(nc.sbuf_tensor(f"G{cc}", [128, WIN], BF16)) for cc in range(8)]
        wo_sb = es5.enter_context(nc.sbuf_tensor("wo_sb", [128, 8, 1024], BF16))
        xsl = [es5.enter_context(nc.sbuf_tensor(f"xsl{i}", [128, 1024], F32)) for i in range(2)]

        sg_wo = DMA("gpsimd", "SG", wo_sb[:, :, :], woT.rearrange("(c p) d -> p c d", p=128))

        W("sync", "SQ", sq_c0)
        W("sync", "CCA", 1)
        vG = sch.inc("SQ", 8 * 16)
        def gload_even(E):
            t0 = E.value_load(off_sb[0:1, 0:1])
            for cc in range(0, 8, 2):
                base = ag1_out[128 * (cc // 2):128 * (cc // 2) + 128, 0:WIN]
                dyn = dataclasses.replace(base, offset=base.offset + t0)
                r = E.dma_start(out=G_sb[cc][:, :], in_=dyn).then_inc(sems["SQ"], 16)
            return r
        sch.emit("sync", gload_even)
        W("sync", "CCB", 1)
        def gload_odd(E):
            t0 = E.value_load(off_sb[0:1, 0:1])
            for cc in range(1, 8, 2):
                base = ag2_out[128 * (cc // 2):128 * (cc // 2) + 128, 0:WIN]
                dyn = dataclasses.replace(base, offset=base.offset + t0)
                r = E.dma_start(out=G_sb[cc][:, :], in_=dyn).then_inc(sems["SQ"], 16)
            return r
        sch.emit("sync", gload_odd)

        x1_dv = [dv_c0, dv_c0]
        x1_dv_t = [0] * NT
        for t in range(NT):
            bi = t % 2
            W("sync", "DV", x1_dv[bi])
            xv = DMA("sync", "SQ", xsl[bi][:, :], x_slice[128 * t:128 * (t + 1), :])
            if t == 0:
                W("tensor", "SQ", vG)
                W("tensor", "SG", sg_wo)
            W("tensor", "DV", x1_dv[(t + 1) % 2])
            def womm(E, t=t):
                for cc in range(8):
                    for k in range(2):
                        r = E.matmul(psD[:, 512 * k:512 * (k + 1)],
                                     G_sb[cc][:, 128 * t:128 * t + 128],
                                     wo_sb[:, cc, 512 * k:512 * (k + 1)],
                                     start=(cc == 0), stop=(cc == 7))
                return r
            pe_p5 = OPI("tensor", "PE", womm)
            W("vector", "PE", pe_p5)
            W("vector", "SQ", xv)
            def x1add(E, t=t, bi=bi):
                E.tensor_add(stream[:, 1024 * t:1024 * t + 512], psD[:, 0:512], xsl[bi][:, 0:512])
                return E.tensor_add(stream[:, 1024 * t + 512:1024 * (t + 1)],
                                    psD[:, 512:1024], xsl[bi][:, 512:1024])
            x1_dv[bi] = OPI("vector", "DV", x1add)
            x1_dv_t[t] = x1_dv[bi]

        def dump_stream(name, ready_sem, ready_val):
            dx = dout(name, [WIN, DM], F32)
            W("sync", ready_sem, ready_val)
            vv = DMA("sync", "SQ", dx.rearrange("(n p) c -> p n c", p=128),
                     stream[:, :].rearrange("p (n c) -> p n c", c=1024))
            W("vector", "SQ", vv)
            W("gpsimd", "SQ", vv)

        if "x1" in dump:
            dump_stream("x1", "DV", max(x1_dv))

        _ffn_id = [0]
        def ffn_phase(ready_sem, ready_vals, esf):
            _ffn_id[0] += 1
            _sfx = f"_f{_ffn_id[0]}"
            def sbf(name, shape, dt):
                return esf.enter_context(nc.sbuf_tensor(name + _sfx, shape, dt))
            ynT = sbf("ynT", [128, 8 * 1280], BF16)
            xmix = sbf("xmix", [128, 8 * 1152], BF16)
            kaT = sbf("kaT", [128, 8 * 1152], BF16)
            krl = sbf("krl", [128, 512], BF16)
            ytile = [sbf(f"ytile{i}", [128, 1024], BF16) for i in range(2)]
            rsig = sbf("rsig", [128, 1024], F32)
            mix_sc = sbf("mix_sc", [128, 1152], F32)
            tv_sb = sbf("tv_sb", [128, 1024], F32)
            Fr_sb = sbf("Fr_sb", [128, 8, 1024], BF16)
            Fk_sb = sbf("Fk_sb", [128, 8, 1024], BF16)
            Fv_sb = sbf("Fv_sb", [128, 8, 1024], BF16)

            W("gpsimd", ready_sem, max(ready_vals) if isinstance(ready_vals, (list, tuple)) else ready_vals)
            DMA("gpsimd", "SG", Fk_sb[:, :, :], FkT.rearrange("(c p) d -> p c d", p=128))
            sg_fkr = DMA("gpsimd", "SG", Fr_sb[:, :, :], FrT.rearrange("(c p) d -> p c d", p=128))
            sg_fv = DMA("gpsimd", "SG", Fv_sb[:, :, :], FvT.rearrange("(c p) d -> p c d", p=128))
            rv = ready_vals if isinstance(ready_vals, (list, tuple)) else [ready_vals] * NT
            for t in range(NT):
                W("vector", ready_sem, rv[t])
                dv_st = ln_stats(stream[:, 1024 * t:1024 * (t + 1)], t, negm_s, var_s)
            W("scalar", "DV", dv_st)
            ac_r = rstd_batch(var_s, rstd_s, NT)
            W("vector", "AC", ac_r)
            yt_free = [0, 0]
            ywv_t = [0] * NT
            for t in range(NT):
                bi = t % 2
                W("vector", "SQ", yt_free[bi])
                dv_a = OPI("vector", "DV", lambda E, t=t, bi=bi: E.tensor_scalar(
                    ytile[bi][:, :], stream[:, 1024 * t:1024 * (t + 1)],
                    negm_s[:, t:t + 1], rstd_s[:, t:t + 1], ALU.add, ALU.mult))
                W("sync", "DV", dv_a)
                ywv_t[t] = DMA("sync", "SQ", yb_d[128 * t:128 * (t + 1), :], ytile[bi][:, :])
                yt_free[bi] = ywv_t[t]

            last_pl = 0
            pl_t = [0] * NT
            pe_sp = 0
            ac_sig_prev = 0
            for sp in range(2):
                lo = 0 if sp == 0 else 1024
                span = 1152 if sp == 0 else 1280
                shoff = 0 if sp == 0 else 128
                W("sync", "SQ", ywv_t[8 if sp == 0 else 17])
                W("sync", "DV", sch.cnt["DV"])
                for cc in range(8):
                    tv_ = DMA("sync", "SQ", ynT[:, 1280 * cc:1280 * cc + span],
                              yb_d[lo:lo + span, 128 * cc:128 * (cc + 1)], transpose=True)
                W("vector", "SQ", tv_)
                W("vector", "PE", pe_sp)

                def mix_one(mu, om, shoff=shoff, sp=sp):
                    r = 0
                    for cc in range(8):
                        b = 1280 * cc
                        o = 1152 * cc
                        if sp == 0:
                            def _a(E, b=b, o=o, mu=mu, om=om):
                                E.tensor_scalar(xmix[:, o:o + 1], ynT[:, b:b + 1],
                                                mu[:, cc if False else 0:1], None, ALU.mult)
                                return E.tensor_scalar(mix_sc[:, 0:1151], ynT[:, b:b + 1151],
                                                       om[:, 0:1], None, ALU.mult)
                            def _a(E, b=b, o=o, mu=mu, om=om, cc=cc):
                                E.tensor_scalar(xmix[:, o:o + 1], ynT[:, b:b + 1],
                                                mu[:, cc:cc + 1], None, ALU.mult)
                                return E.tensor_scalar(mix_sc[:, 0:1151], ynT[:, b:b + 1151],
                                                       om[:, cc:cc + 1], None, ALU.mult)
                            OPIW("vector", "DV", _a)
                            def _b(E, b=b, o=o, mu=mu, cc=cc):
                                return E.scalar_tensor_tensor(xmix[:, o + 1:o + 1152],
                                                              ynT[:, b + 1:b + 1152],
                                                              mu[:, cc:cc + 1], mix_sc[:, 0:1151],
                                                              ALU.mult, ALU.add)
                            r = OPI("vector", "DV", _b)
                        else:
                            def _a(E, b=b, mu=mu, om=om, cc=cc, shoff=shoff):
                                return E.tensor_scalar(mix_sc[:, 0:1152],
                                                       ynT[:, b + shoff - 1:b + shoff - 1 + 1152],
                                                       om[:, cc:cc + 1], None, ALU.mult)
                            OPIW("vector", "DV", _a)
                            def _b(E, b=b, o=o, mu=mu, cc=cc, shoff=shoff):
                                return E.scalar_tensor_tensor(xmix[:, o:o + 1152],
                                                              ynT[:, b + shoff:b + shoff + 1152],
                                                              mu[:, cc:cc + 1], mix_sc[:, 0:1152],
                                                              ALU.mult, ALU.add)
                            r = OPI("vector", "DV", _b)
                    return r

                dv_mx = mix_one(mu_k, om_k)
                W("tensor", "DV", dv_mx)
                W("tensor", "AC", ac_sig_prev)
                if sp == 0:
                    W("tensor", "SG", sg_fkr)
                ac_hist = [0, 0]
                dv_k = 0
                for dd in range(8):
                    for ncb in range(3):
                        n0 = 512 * ncb
                        nw = min(512, 1152 - n0)
                        slot = (dd * 3 + ncb) % 2
                        pslc = psB[:, 0:nw] if slot == 0 else psA[:, 0:nw]
                        W("tensor", "AC", ac_hist[slot])
                        def kmm(E, dd=dd, n0=n0, nw=nw, pslc=pslc):
                            for cc in range(8):
                                r = E.matmul(pslc, Fk_sb[:, cc, 128 * dd:128 * dd + 128],
                                             xmix[:, 1152 * cc + n0:1152 * cc + n0 + nw],
                                             start=(cc == 0), stop=(cc == 7))
                            return r
                        pe_k = OPI("tensor", "PE", kmm)
                        W("scalar", "PE", pe_k)
                        W("scalar", "DV", dv_k)
                        ac_k = OPI("scalar", "AC", lambda E, pslc=pslc, nw=nw: E.activation(
                            krl[:, 0:nw], pslc, AF.Relu, bias=0.0, scale=1.0))
                        ac_hist[slot] = ac_k
                        W("vector", "AC", ac_k)
                        dv_k = OPI("vector", "DV", lambda E, dd=dd, n0=n0, nw=nw: E.tensor_mul(
                            kaT[:, 1152 * dd + n0:1152 * dd + n0 + nw],
                            krl[:, 0:nw], krl[:, 0:nw]))
                pe_kdone = sch.cnt["PE"]
                W("vector", "PE", pe_kdone)
                dv_mr = mix_one(mu_r, om_r)
                W("tensor", "DV", dv_k)
                W("tensor", "DV", dv_mr)
                if sp == 0:
                    W("tensor", ready_sem, max(rv))
                    W("tensor", "SG", sg_fv)
                ac_sig = 0
                ac_sig_h = [0, 0]
                dv_rv = 0
                for tl in range(9):
                    t = 9 * sp + tl
                    rp = psD[:, 0:1024] if t % 2 == 0 else psA[:, 0:1024]
                    W("tensor", "AC", ac_sig_h[t % 2])
                    def rmm(E, tl=tl, rp=rp):
                        for cc in range(8):
                            for k in range(2):
                                r = E.matmul(rp[:, 512 * k:512 * (k + 1)],
                                             xmix[:, 1152 * cc + 128 * tl:1152 * cc + 128 * tl + 128],
                                             Fr_sb[:, cc, 512 * k:512 * (k + 1)],
                                             start=(cc == 0), stop=(cc == 7))
                        return r
                    pe_r = OPI("tensor", "PE", rmm)
                    W("scalar", "PE", pe_r)
                    W("scalar", "DV", dv_rv)
                    def sig(E, rp=rp):
                        E.activation(rsig[:, 0:512], rp[:, 0:512], AF.Sigmoid, bias=0.0, scale=1.0)
                        return E.activation(rsig[:, 512:1024], rp[:, 512:1024], AF.Sigmoid, bias=0.0, scale=1.0)
                    ac_sig = OPI("scalar", "AC", sig)
                    ac_sig_h[t % 2] = ac_sig
                    ac_sig_prev = ac_sig
                    W("tensor", "DV", dv_rv)
                    def vmm2(E, tl=tl):
                        for cc in range(8):
                            for k in range(2):
                                r = E.matmul(psA[:, 1024 + 512 * k:1024 + 512 * (k + 1)],
                                             kaT[:, 1152 * cc + 128 * tl:1152 * cc + 128 * tl + 128],
                                             Fv_sb[:, cc, 512 * k:512 * (k + 1)],
                                             start=(cc == 0), stop=(cc == 7))
                        return r
                    pe_v = OPI("tensor", "PE", vmm2)
                    W("vector", "PE", pe_v)
                    W("vector", "AC", ac_sig)
                    W("vector", "PL", last_pl)
                    def rv(E):
                        E.tensor_mul(tv_sb[:, 0:512], psA[:, 1024:1536], rsig[:, 0:512])
                        return E.tensor_mul(tv_sb[:, 512:1024], psA[:, 1536:2048], rsig[:, 512:1024])
                    dv_rv = OPI("vector", "DV", rv)
                    W("gpsimd", "DV", dv_rv)
                    W("gpsimd", "PL", last_pl)
                    last_pl = OPI("gpsimd", "PL", lambda E, t=t: E.tensor_add(
                        stream[:, 1024 * t:1024 * (t + 1)],
                        stream[:, 1024 * t:1024 * (t + 1)], tv_sb[:, :]))
                    pl_t[t] = last_pl
                pe_sp = sch.cnt["PE"]
            return pl_t

        es5.close()
        with contextlib.ExitStack() as es6:
            pl_x2_t = ffn_phase("DV", x1_dv_t, es6)
            pl_x2 = max(pl_x2_t)
        if "x2" in dump:
            dump_stream("x2", "PL", pl_x2)

        es7 = contextlib.ExitStack()
        def sb7(name, shape, dt):
            return es7.enter_context(nc.sbuf_tensor(name, shape, dt))
        y3n = sb7("y3n", [128, NT * 1024], BF16)
        y3nT = sb7("y3nT", [128, 8 * 1408], BF16)
        AT_sb = sb7("AT_sb", [128, 8 * 1408], BF16)
        bmask = sb7("bmask_sb", [128, 3, 384], BF16)
        sc_sb = sb7("sc_sb", [128, 384], F32)
        negb = sb7("negb", [128, 1], F32)
        rcpb = sb7("rcpb", [128, 1], F32)
        EbT = sb7("EbT", [128, 384], BF16)
        ra_sb = sb7("ra_sb", [128, 1024], F32)
        y3f = sb7("y3f", [128, 1024], F32)
        Wb_sb = sb7("Wb_sb", [128, 8, 1024], BF16)

        sg_wb = DMA("gpsimd", "SG", Wb_sb[:, :, :], WbI.rearrange("(c p) d -> p c d", p=128))
        W("sync", "PL", pl_x2)
        bmv = DMA("sync", "SQ", bmask[:, :, :], bmask_in.transpose([1, 0, 2]))

        for t in range(NT):
            W("vector", "PL", pl_x2_t[t])
            dv_st3 = ln_stats(stream[:, 1024 * t:1024 * (t + 1)], t, negm3, var_s)
        W("scalar", "DV", dv_st3)
        ac_r3 = rstd_batch(var_s, rstd3, NT)
        W("vector", "AC", ac_r3)
        ywv_t7 = [0] * NT
        for t in range(NT):
            dv_a = OPI("vector", "DV", lambda E, t=t: E.tensor_scalar(
                y3n[:, 1024 * t:1024 * (t + 1)], stream[:, 1024 * t:1024 * (t + 1)],
                negm3[:, t:t + 1], rstd3[:, t:t + 1], ALU.add, ALU.mult))
            W("sync", "DV", dv_a)
            ywv_t7[t] = DMA("sync", "SQ", yb_d[128 * t:128 * (t + 1), :], y3n[:, 1024 * t:1024 * (t + 1)])

        pe_sp7 = 0
        dv_ra = 0
        ac_ebt = 0
        pe_t7 = 0
        dv_sc = 0
        pl_x3 = 0
        pl_x3_t = [0] * NT
        for sp in range(2):
            lo = 0 if sp == 0 else 1024
            span = 1408 if sp == 0 else 1280
            W("sync", "SQ", ywv_t7[10 if sp == 0 else 17])
            W("sync", "PE", pe_sp7)
            for cc in range(8):
                tv_ = DMA("sync", "SQ", y3nT[:, 1408 * cc:1408 * cc + span],
                          yb_d[lo:lo + span, 128 * cc:128 * (cc + 1)], transpose=True)
            W("tensor", "SQ", tv_)
            if sp == 0:
                W("tensor", "SG", sg_wb)
            ac_hist = [0, 0]
            ac_a = 0
            nchunks = [(0, 512), (512, 512), (1024, span - 1024)]
            for ee in range(8):
                for ci, (n0, nw) in enumerate(nchunks):
                    slot = (ee * 3 + ci) % 2
                    pslc = psB[:, 0:nw] if slot == 0 else psA[:, 0:nw]
                    W("tensor", "AC", ac_hist[slot])
                    def amm(E, ee=ee, n0=n0, nw=nw, pslc=pslc):
                        for cc in range(8):
                            r = E.matmul(pslc, Wb_sb[:, cc, 128 * ee:128 * ee + 128],
                                         y3nT[:, 1408 * cc + n0:1408 * cc + n0 + nw],
                                         start=(cc == 0), stop=(cc == 7))
                        return r
                    pe_a = OPI("tensor", "PE", amm)
                    W("scalar", "PE", pe_a)
                    ac_a = OPI("scalar", "AC", lambda E, ee=ee, n0=n0, nw=nw, pslc=pslc: E.copy(
                        AT_sb[:, 1408 * ee + n0:1408 * ee + n0 + nw], pslc))
                    ac_hist[slot] = ac_a
            if sp == 0:
                W("tensor", "SQ", bmv)
            qts = range(0, 9) if sp == 0 else range(9, 18)
            for t in qts:
                w0 = 0 if t == 0 else (1920 if t == 17 else 128 * (t - 1))
                mv = 0 if t == 0 else (2 if t == 17 else 1)
                W("tensor", "DV", dv_sc)
                W("tensor", "AC", ac_a)
                def smm7(E, t=t, w0=w0, lo=lo):
                    for cc in range(8):
                        r = E.matmul(psB[:, 0:384],
                                     AT_sb[:, 1408 * cc + 128 * t - lo:1408 * cc + 128 * t - lo + 128],
                                     y3nT[:, 1408 * cc + w0 - lo:1408 * cc + w0 - lo + 384],
                                     start=(cc == 0), stop=(cc == 7))
                    return r
                pe_s7 = OPI("tensor", "PE", smm7)
                W("vector", "PE", pe_s7)
                OPIW("vector", "DV", lambda E, mv=mv: E.tensor_add(
                    sc_sb[:, :], psB[:, 0:384], bmask[:, mv, :]))
                dv_sc = OPI("vector", "DV", lambda E: E.reduce_max(
                    negb[:, :], sc_sb[:, :], axis=AXX, negate=True))
                W("scalar", "DV", dv_sc)
                W("scalar", "PE", pe_t7)
                ac_e7 = OPI("scalar", "AC", lambda E: E.activation(
                    etri[:, 0:384], sc_sb[:, :], AF.Exp, bias=negb[:, :], scale=1.0,
                    accum_out=rcpb[:, :]))
                W("vector", "AC", ac_e7)
                dv_rc = OPI("vector", "DV", lambda E: E.reciprocal(rcpb[:, :], rcpb[:, :]))
                W("tensor", "AC", ac_e7)
                W("tensor", "AC", ac_ebt)
                def tmm(E):
                    for kbl in range(3):
                        r = E.transpose(psE[:, 128 * kbl:128 * (kbl + 1)],
                                        etri[:, 128 * kbl:128 * (kbl + 1)], ident[:, :])
                    return r
                pe_t7 = OPI("tensor", "PE", tmm)
                W("scalar", "PE", pe_t7)
                ac_ebt = OPI("scalar", "AC", lambda E: E.copy(EbT[:, 0:384], psE[:, 0:384]))
                W("tensor", "AC", ac_ebt)
                W("tensor", "DV", dv_ra)
                def pv7(E, w0=w0):
                    kt0 = w0 // 128
                    for kbl in range(3):
                        for k in range(2):
                            r = E.matmul(psD[:, 512 * k:512 * (k + 1)],
                                         EbT[:, 128 * kbl:128 * kbl + 128],
                                         y3n[:, 1024 * (kt0 + kbl) + 512 * k:1024 * (kt0 + kbl) + 512 * (k + 1)],
                                         start=(kbl == 0), stop=(kbl == 2))
                    return r
                pe_pv7 = OPI("tensor", "PE", pv7)
                W("vector", "PE", pe_pv7)
                W("vector", "DV", dv_rc)
                W("vector", "PL", pl_x3)
                def rafin(E, t=t):
                    E.tensor_scalar(ra_sb[:, 0:512], psD[:, 0:512], rcpb[:, :], 0.0, ALU.mult, ALU.max)
                    E.tensor_scalar(ra_sb[:, 512:1024], psD[:, 512:1024], rcpb[:, :], 0.0, ALU.mult, ALU.max)
                    return E.tensor_scalar(y3f[:, :], stream[:, 1024 * t:1024 * (t + 1)],
                                           negm3[:, t:t + 1], rstd3[:, t:t + 1], ALU.add, ALU.mult)
                dv_ra = OPI("vector", "DV", rafin)
                W("gpsimd", "DV", dv_ra)
                W("gpsimd", "PL", pl_x3)
                def x3add(E, t=t):
                    E.tensor_add(stream[:, 1024 * t:1024 * (t + 1)],
                                 stream[:, 1024 * t:1024 * (t + 1)], y3f[:, :])
                    return E.tensor_add(stream[:, 1024 * t:1024 * (t + 1)],
                                        stream[:, 1024 * t:1024 * (t + 1)], ra_sb[:, :])
                pl_x3 = OPI("gpsimd", "PL", x3add)
                pl_x3_t[t] = pl_x3
            pe_sp7 = sch.cnt["PE"]

        if "x3" in dump:
            dump_stream("x3", "PL", pl_x3)

        es7.close()
        with contextlib.ExitStack() as es8:
            pl_x4_t = ffn_phase("PL", pl_x3_t, es8)

        ov = 0
        for t in range(NT):
            W("sync", "PL", pl_x4_t[t])
            ov = DMA("sync", "SQ", out[128 * t:128 * (t + 1), :], stream[:, 1024 * t:1024 * (t + 1)])
        W("sync", "SQ", ov)

        @blockctx.tensor
        def _(E):
            for fn in sch.ops["tensor"]:
                fn(E)

        @blockctx.scalar
        def _(E):
            for fn in sch.ops["scalar"]:
                fn(E)

        @blockctx.vector
        def _(E):
            for fn in sch.ops["vector"]:
                fn(E)

        @blockctx.gpsimd
        def _(E):
            for fn in sch.ops["gpsimd"]:
                fn(E)

        @blockctx.sync
        def _(E):
            for fn in sch.ops["sync"]:
                fn(E)

    return nc


def _rope_tables():
    inv = (1.0 / (np.float32(10000.0) ** (np.arange(0, 64, 2, dtype=np.float32) / np.float32(64.0)))).astype(np.float32)
    ang = np.arange(S, dtype=np.float32)[:, None] * inv[None, :]
    cos = np.cos(ang).astype(np.float32)
    sin = np.sin(ang).astype(np.float32)
    cosT = np.empty((128, S), np.float32)
    sgnT = np.empty((128, S), np.float32)
    for r in range(128):
        j = r % 32
        cosT[r] = cos[:, j]
        sgnT[r] = (-sin[:, j]) if (r % 64) < 32 else sin[:, j]
    bf = ml_dtypes.bfloat16
    return np.ascontiguousarray(cosT.astype(bf)), np.ascontiguousarray(sgnT.astype(bf))


def _perm_rows(h0):
    n, r = [], []
    for h in range(h0, h0 + 4):
        ev = [64 * h + 2 * j for j in range(32)]
        od = [64 * h + 2 * j + 1 for j in range(32)]
        n += ev + od
        r += od + ev
    return np.array(n), np.array(r)


def _bmasks():
    m = np.zeros((3, 128, 384), np.float32)
    for v, d0 in ((0, 0), (1, 128), (2, 256)):
        i = np.arange(128)[:, None]
        j = np.arange(384)[None, :]
        delta = d0 + i - j
        m[v] = np.where(np.abs(delta) <= 64, 0.0, -1e9)
    return np.ascontiguousarray(m.astype(ml_dtypes.bfloat16))


def kernel(L0_attn_a_x, L0_attn_a_wq, L0_attn_a_wk, L0_attn_a_wv, L0_attn_a_wo,
           L0_attn_b_W, L0_ffn_W_r, L0_ffn_W_k, L0_ffn_W_v, L0_ffn_mu_r, L0_ffn_mu_k):
    x = np.ascontiguousarray(np.asarray(L0_attn_a_x, dtype=np.float32))
    wq = np.asarray(L0_attn_a_wq, np.float32)
    wk = np.asarray(L0_attn_a_wk, np.float32)
    wv = np.asarray(L0_attn_a_wv, np.float32)
    wo = np.asarray(L0_attn_a_wo, np.float32)
    Wb = np.asarray(L0_attn_b_W, np.float32)
    Fr = np.asarray(L0_ffn_W_r, np.float32)
    Fk = np.asarray(L0_ffn_W_k, np.float32)
    Fv = np.asarray(L0_ffn_W_v, np.float32)
    mu_r = np.asarray(L0_ffn_mu_r, np.float32)
    mu_k = np.asarray(L0_ffn_mu_k, np.float32)

    bf = ml_dtypes.bfloat16
    cosT, sgnT = _rope_tables()
    tri = np.ascontiguousarray(
        (np.arange(128)[:, None] <= np.arange(128)[None, :]).astype(np.float32).astype(bf))
    bm = _bmasks()
    ident = np.ascontiguousarray(np.eye(128, dtype=np.float32).astype(bf))
    woT = np.ascontiguousarray(wo.T)
    FrT = np.ascontiguousarray(Fr.T)
    FkT = np.ascontiguousarray(Fk.T)
    FvT = np.ascontiguousarray(Fv.T)
    mu_r2 = np.ascontiguousarray(mu_r.reshape(8, 128).T)
    mu_k2 = np.ascontiguousarray(mu_k.reshape(8, 128).T)

    in_maps = []
    for c in range(8):
        b, g = c // 4, c % 4
        off = OFFS[g]
        rn, rr = _perm_rows(4 * g)
        in_maps.append({
            "x_full": x[b],
            "x_slice": np.ascontiguousarray(x[b, off:off + WIN]),
            "wqT_n": np.ascontiguousarray(wq[rn].T),
            "wqT_r": np.ascontiguousarray(wq[rr].T),
            "wkT_n": np.ascontiguousarray(wk[rn].T),
            "wkT_r": np.ascontiguousarray(wk[rr].T),
            "wvT": np.ascontiguousarray(wv[4 * g * 64:(4 * g + 4) * 64].T),
            "woT": woT,
            "FrT": FrT, "FkT": FkT, "FvT": FvT,
            "Wb": Wb,
            "mu_r": mu_r2, "mu_k": mu_k2,
            "rope_cos": cosT, "rope_sgn": sgnT,
            "tri": tri, "bmask": bm, "ident": ident,
            "goff": np.array([[off]], dtype=np.int32),
        })

    nc = build_program()
    res = run_bass_kernel_spmd(nc, in_maps, core_ids=list(range(8)))
    global _last_res
    _last_res = res

    outp = np.empty((2, S, DM), np.float32)
    for c in range(8):
        b, g = c // 4, c % 4
        o = np.asarray(res.results[c]["out"], np.float32)
        outp[b, 2048 * g:2048 * (g + 1)] = o[R0[g]:R0[g] + 2048]
    return outp
